# version 32
# baseline (speedup 1.0000x reference)
import contextlib
import heapq
import os
import numpy as np
from ml_dtypes import bfloat16

import concourse.bass as bass
import concourse.mybir as mybir
from concourse.bass_utils import run_bass_kernel_spmd

F32 = mybir.dt.float32
BF16 = mybir.dt.bfloat16
AF = mybir.ActivationFunctionType
ALU = mybir.AluOpType

NCORES = 8
D = 1024
H = 16
NG = 8
CH = 512
NJ = 8
NT = 4
CG_ITERS = 15
BIGNEG = -30000.0
FCONST_VALS = (1e-5, 1.0, 1e-6, -1.0, -0.5, 0.0, 0.5, 1e-12)


class KB:

    ENGS = ("v", "s", "t", "p", "d")

    def __init__(self, nc):
        self.nc = nc
        self.q = {e: [] for e in self.ENGS}
        self.cnt = dict.fromkeys(self.ENGS, 0)
        self.hw = {}
        self.sem = {}
        self.reg = {}
        self.wcnt = [0] * 12
        self.cap = None

    def region(self, name, psum=False):
        self.reg[name] = {"w": None, "r": [], "psum": psum}
        return name

    def _wait(self, e, mark):
        if mark is None:
            return
        pe, val = mark
        if pe == e and val <= 0:
            return
        if pe == "d":
            val = self.cnt["d"]
        key = (e, pe)
        if self.hw.get(key, 0) >= val:
            return
        self.hw[key] = val
        sem = self.sem[pe]
        self.q[e].append(lambda eng, s=sem, v=val: eng.wait_ge(s, v))

    def op(self, e, fn, reads=(), writes=(), extra=(), out=None):
        if self.cap is not None:
            self.cap.append((e, None, fn, tuple(reads), tuple(writes), out))
            return None
        for m in extra:
            self._wait(e, m)
        for rn in reads:
            r = self.reg[rn]
            self._wait(e, r["w"])
        for wn in writes:
            r = self.reg[wn]
            wm = r["w"]
            if wm is not None and not (r["psum"] and e == "t" and wm[0] == "t"):
                self._wait(e, wm)
            for m in r["r"]:
                if r["psum"] and e == "t" and m[0] == "t":
                    continue
                self._wait(e, m)
        step = 16 if e == "d" else 1
        self.cnt[e] += step
        mark = (e, self.cnt[e])
        sem = self.sem[e]
        self.q[e].append(lambda eng, f=fn, s=sem, n=step: f(eng).then_inc(s, n))
        for rn in reads:
            self.reg[rn]["r"].append(mark)
        for wn in writes:
            r = self.reg[wn]
            r["w"] = mark
            r["r"] = []
        return mark

    def wdma(self, fn, slot, reads=(), writes=(), out=None, q="d"):
        if self.cap is not None:
            self.cap.append(("w" + q, slot, fn, tuple(reads), tuple(writes),
                             out))
            return None
        e = q
        for rn in reads:
            self._wait(e, self.reg[rn]["w"])
        for wn in writes:
            r = self.reg[wn]
            if r["w"] is not None:
                self._wait(e, r["w"])
            for m in r["r"]:
                self._wait(e, m)
        key = "w%d" % slot
        self.wcnt[slot] += 16
        mark = (key, self.wcnt[slot])
        sem = self.sem[key]
        self.q[e].append(lambda eng, f=fn, s=sem: f(eng).then_inc(s, 16))
        for rn in reads:
            self.reg[rn]["r"].append(mark)
        for wn in writes:
            r = self.reg[wn]
            r["w"] = mark
            r["r"] = []
        return mark

    def emit(self, blk):
        for e, sec in (("d", blk.sync), ("v", blk.vector), ("s", blk.scalar),
                       ("t", blk.tensor), ("p", blk.gpsimd)):
            ops = self.q[e]

            def body(eng, ops=ops):
                for f in ops:
                    f(eng)
            sec(body)


SCHEDULE = True
_SCHED_WIN = float(os.environ.get("SCHED_WIN", "0.01"))
_SCHED_HOP = float(os.environ.get("SCHED_HOP", "0.1"))
_SCHED_TK = float(os.environ.get("SCHED_TK", "0.03"))
_SCHED_TC = float(os.environ.get("SCHED_TC", "0.0008"))


def _schedule(recs):
    n = len(recs)
    eng = [None] * n
    dur = [0.0] * n
    flight = [0.0] * n
    isdma = [False] * n
    for i, (e, slot, fn, reads, writes, out) in enumerate(recs):
        ee = e[1] if e.startswith("w") else e
        eng[i] = ee
        cols, nbytes = 512, 262144
        if out is not None:
            try:
                sh = out.shape
                rows = sh[0]
                cols = 1
                for s in sh[1:]:
                    cols *= s
                nbytes = rows * cols * (2 if "bf" in str(out.dtype) else 4)
            except Exception:
                pass
        if ee == "v":
            dur[i] = 0.10 + cols * 0.00095
        elif ee == "s":
            if e == "ws":
                dur[i] = 0.30
                flight[i] = 1.5 + nbytes * 2e-5
            else:
                dur[i] = 0.18 + cols * 0.00095
        elif ee == "p":
            dur[i] = 0.30 + cols * 0.0018
        elif ee == "t":
            dur[i] = _SCHED_TK + cols * _SCHED_TC
        else:
            dur[i] = 0.30
            flight[i] = 1.5 + nbytes * 2e-5
            isdma[i] = e == "d"
    preds = [set() for _ in range(n)]
    succs = [set() for _ in range(n)]
    lastw = {}
    readers = {}
    for i, (e, slot, fn, reads, writes, out) in enumerate(recs):
        for rn in reads:
            j = lastw.get(rn)
            if j is not None and j != i:
                preds[i].add(j)
                succs[j].add(i)
        for wn in writes:
            j = lastw.get(wn)
            if j is not None and j != i:
                preds[i].add(j)
                succs[j].add(i)
            for rd in readers.get(wn, ()):
                if rd != i:
                    preds[i].add(rd)
                    succs[rd].add(i)
        for rn in reads:
            readers.setdefault(rn, []).append(i)
        for wn in writes:
            lastw[wn] = i
            readers[wn] = []
    prio = [0.0] * n
    for i in range(n - 1, -1, -1):
        m = 0.0
        for k in succs[i]:
            if prio[k] > m:
                m = prio[k]
        prio[i] = m + dur[i]
    hasdp = [any(isdma[j] for j in preds[i]) for i in range(n)]
    nleft = [len(preds[i]) for i in range(n)]
    dready = [0.0] * n
    free = dict.fromkeys("vstpd", 0.0)
    heaps = {e: [] for e in "vstpd"}
    for i in range(n):
        if nleft[i] == 0:
            heapq.heappush(heaps[eng[i]], (0.0, -prio[i], i))
    dmax = 0.0
    order = []
    for _ in range(n):
        best = None
        for e in "vstpd":
            h = heaps[e]
            if not h:
                continue
            t0 = max(free[e], h[0][0])
            tmp = []
            cand = None
            while h and h[0][0] <= t0 + _SCHED_WIN:
                it = heapq.heappop(h)
                tmp.append(it)
                if cand is None or it[1] < cand[1]:
                    cand = it
            for it in tmp:
                if it is not cand:
                    heapq.heappush(h, it)
            st = max(free[e], cand[0])
            if best is None or (st, cand[1]) < (best[0], best[1][1]):
                if best is not None:
                    heapq.heappush(heaps[best[2]], best[1])
                best = (st, cand, e)
            else:
                heapq.heappush(heaps[e], cand)
        st, (dr, nprio, i), e = best
        if hasdp[i]:
            st = max(st, dmax)
        fin = st + dur[i]
        free[e] = fin
        comp = fin + flight[i]
        if isdma[i]:
            dmax = max(dmax, comp)
        order.append(i)
        for k in succs[i]:
            lat = 0.0 if (eng[i] == eng[k] or eng[i] == "d") else _SCHED_HOP
            if comp + lat > dready[k]:
                dready[k] = comp + lat
            nleft[k] -= 1
            if nleft[k] == 0:
                heapq.heappush(heaps[eng[k]], (dready[k], -prio[k], k))
    _schedule.stats = {"makespan": max(free.values()),
                       "busy": {e: round(sum(dur[i] for i in range(n)
                                             if eng[i] == e), 1)
                                for e in "vstpd"}}
    return order


def build_nc(T=4096):
    NCH = T // CH
    nc = bass.Bass("TRN2", target_bir_lowering=False)
    K = KB(nc)
    R = K.region

    def din(name, shape, dt=F32):
        return nc.dram_tensor(name, shape, dt, kind="ExternalInput").ap()

    x_d = din("x", [T, D])
    wq_d = din("Wq", [D, D]); wk_d = din("Wk", [D, D]); wv_d = din("Wv", [D, D])
    wa_d = din("Wa", [D, H]); wb_d = din("Wb", [D, H])
    bb_d = din("bb", [H, 1]); alog_d = din("A_log", [H, 1]); dtb_d = din("dt_bias", [H, 1])
    lam_d = din("lambda_params", [D, 1])
    wcq_d = din("Wconv_q", [D, 4]); wck_d = din("Wconv_k", [D, 4])
    nrmw_d = din("norm_w", [128, 1])
    wo_d = din("Wo", [D, D])
    wf_d = din("W_ffn", [D, D]); bf_d = din("b_ffn", [1, D])
    ld_d = din("lora_down", [4, 100]); lu_d = din("lora_up", [50, 4])
    da_d = din("down_aux", [100, D]); ua_d = din("up_aux", [D, 50])
    idf_d = din("IDF", [128, 128])
    idb_d = din("IDB", [128, 128], BF16)
    sel_d = din("SEL", [64, 128], BF16)
    trirow_d = din("TRIROW", [64, CH], BF16)
    ind_d = din("IND", [128, 128], BF16)
    inde_d = din("INDE", [16, NG * 128], BF16)
    ones1_d = din("ONES1", [1, 128], BF16)
    epsrow_d = din("EPSROW", [1, CH], BF16)
    indl_d = din("INDL", [128, 8], BF16)
    eloc_d = din("ELOC", [4, 256], BF16)
    fconst_d = din("FCONST", [128, 8])
    out_d = nc.dram_tensor("out", [T, D], F32, kind="ExternalOutput").ap()
    wqkvT_s = nc.dram_tensor("WQKVT_BF", [24 * 128, D], BF16, kind="Internal").ap()
    wo_s = nc.dram_tensor("WO_BF", [D, D], BF16, kind="Internal").ap()
    weff_s = nc.dram_tensor("WEFF_BF", [D, D], BF16, kind="Internal").ap()

    es = contextlib.ExitStack()

    def sb(name, shape, dt=F32):
        return es.enter_context(nc.sbuf_tensor(name, shape, dt))

    with es:
        idf = sb("idf", [128, 128]); idb = sb("idb", [128, 128], BF16)
        sel = sb("sel", [64, 128], BF16); trirow = sb("trirow", [64, CH], BF16)
        ind = sb("ind", [128, 128], BF16); inde = sb("inde", [16, NG * 128], BF16)
        ones1 = sb("ones1", [1, 128], BF16); epsrow = sb("epsrow", [1, CH], BF16)
        indl = sb("indl", [128, 8], BF16); eloc = sb("eloc", [4, 256], BF16)
        fconst = sb("fconst", [128, 8])
        nrmw = sb("nrmw", [128, 1])
        bbc = sb("bbc", [H, 1]); negea = sb("negea", [H, 1]); dtbc = sb("dtbc", [H, 1])
        lamg = sb("lamg", [128, NG])
        lamtile = sb("lamtile", [128, NG * 128], BF16)
        wcq = sb("wcq", [128, NG * 4]); wck = sb("wck", [128, NG * 4])
        brow = sb("brow", [1, D], BF16)
        colsc = sb("colsc", [128, 16])
        wmt = sb("wmt", [128, 2048], BF16)
        wab_sb = sb("wab_sb", [128, 256], BF16)
        wrhs = sb("wrhs", [128, 3 * 1024], BF16)
        stg = sb("stg", [128, 1344])
        stg2 = sb("stg2", [128, D])
        xn = sb("xn", [128, NG * CH], BF16)
        qkraw = sb("qkraw", [128, 2 * (CH + 4)], BF16)
        qhalo = sb("qhalo", [128, NG * 3], BF16)
        khalo = sb("khalo", [128, NG * 3], BF16)
        vfmb = sb("vfmb", [128, CH + 128], BF16)
        sqbf = sb("sqbf", [128, 1024])
        sqb = sqbf[:].bitcast(BF16)
        knfm = sb("knfm", [128, NG * (CH + 128)], BF16)
        ktm = sb("ktm", [128, NG * 9 * 128], BF16)
        vtm = sb("vtm", [128, NG * 9 * 128], BF16)
        khsc = sb("khsc", [128, 512], BF16)
        masks = sb("masks", [128, NG * CH], BF16)
        ebc = sb("ebc", [128, NG * CH], BF16)
        bsn = sb("bsn", [128, 2 * NG * 9 * 128], BF16)
        pst = sb("pst", [128, NG * CH], BF16)
        rst = sb("rst", [128, NG * CH], BF16)
        xst = sb("xst", [128, NG * CH], BF16)
        onb = sb("onb", [128, NG * CH], BF16)
        csc = sb("csc", [128, 2 * CH], BF16)
        ptl = sb("ptl", [128, 2 * CH], BF16)
        q1b = sb("q1b", [128, 4 * CH], BF16)
        abcs = sb("abcs", [128, 2 * CH], BF16)
        land = sb("land", [16, 8 * CH])
        zeros16 = sb("zeros16", [16, 64])
        landb = sb("landb", [16, 4 * CH], BF16)
        outsb = sqbf[:, 0:CH]
        csT = sb("csT", [128, 64])
        facT = sb("facT", [128, 64])
        csS = sb("csS", [128, NG * 8])
        pq = [es.enter_context(nc.psum_tensor(f"pq{i}", [128, 1024], F32))
              for i in range(4)]
        pb = [pq[i // 2][:, (i % 2) * 512:(i % 2) * 512 + 512] for i in range(8)]
        pbb = [p.bitcast(BF16) for p in pb]

        sems = {e: es.enter_context(nc.semaphore(f"sem_{e}")) for e in KB.ENGS}
        for i in range(12):
            sems[f"w{i}"] = es.enter_context(nc.semaphore(f"sem_w{i}"))
        K.sem = sems
        blk = es.enter_context(nc.Block())

        PB = [R(f"pb{i}", psum=True) for i in range(8)]
        CONSTS = R("consts")
        rCOL = R("colsc"); rLANDB = R("landb")
        rLND = [R(f"land{i}") for i in range(8)]
        rCST = R("csT"); rFAC = R("facT"); rCSS = R("csS")
        rRCQ = [R("rcq0"), R("rcq1"), R("rcq2"), R("rcq3")]
        rAL = [R("al0"), R("al1")]
        rSTG = R("stg"); rSTG2 = R("stg2")
        rSTGQ = [R("stgq0"), R("stgq1")]
        rWMT = [R("wmt0"), R("wmt1"), R("wmt2")]
        rWAB = R("wab_sb")
        rWRHS = [R(f"wrhs{i}") for i in range(3)]; rBROW = R("brow")
        rXN = [R(f"xn{k}") for k in range(8)]
        rQK = [R("qkraw0"), R("qkraw1")]
        rSQ2 = [R("sq0"), R("sq1")]
        rHAL = R("halos")
        rVF = [R("vfmb0")]
        rKN = [R(f"knfm{g}") for g in range(NG)]
        rKT = [R(f"ktm{g}") for g in range(NG)]
        rVT = [R(f"vtm{g}") for g in range(NG)]
        rKH = [R(f"khsc{i}") for i in range(4)]
        rMSK = [R(f"masks{g}") for g in range(NG)]
        rEBC = [R(f"ebc{g}") for g in range(NG)]
        rBSN = [[R(f"bsn{kv}_{g}") for g in range(NG)] for kv in range(2)]
        rP = [R(f"pst{g}") for g in range(NG)]
        rRS = [R(f"rst{g}") for g in range(NG)]
        rX = [R(f"xst{g}") for g in range(NG)]
        rON = [R(f"onb{g}") for g in range(NG)]
        rCSC = [R("csc0"), R("csc0x"), R("cscq")]
        rPTL = [R("ptl0"), R("ptl1")]
        rQ1 = [R("q1b0"), R("q1b1")]
        rTR = rPTL
        rTX = rQ1
        rABC = [R("abcs0"), R("abcs1")]
        rLAM = R("lamtile"); rWST = R("wstage")

        def dma(dst, src, reads=(), writes=(), extra=()):
            return K.op("d", lambda e: e.dma_start(out=dst, in_=src),
                        reads, writes, extra, out=dst)

        def wdma(dst, src, slot, reads=(), writes=()):
            return K.wdma(lambda e: e.dma_start(out=dst, in_=src), slot,
                          reads, writes, out=dst)

        def mm(out, lhsT, rhs, start, stop, reads=(), writes=(), tp=None):
            if tp is None:
                return K.op("t", lambda e: e.matmul(out=out, lhsT=lhsT, rhs=rhs,
                                                    start=start, stop=stop,
                                                    skip_group_check=True),
                            reads, writes, out=out)
            return K.op("t", lambda e: e.matmul(out=out, lhsT=lhsT, rhs=rhs,
                                                start=start, stop=stop,
                                                skip_group_check=True,
                                                tile_position=tp),
                        reads, writes, out=out)

        def tr(out, in_, ident, reads=(), writes=()):
            return K.op("t", lambda e: e.transpose(out=out, in_=in_, identity=ident),
                        tuple(reads) + (CONSTS,), writes, out=out)

        def vtt(out, a, b, op, reads=(), writes=()):
            return K.op("v", lambda e: e.tensor_tensor(out=out, in0=a, in1=b, op=op),
                        reads, writes, out=out)

        def vts(out, a, s1, op0, s2=None, op1=None, reads=(), writes=()):
            if op1 is None:
                return K.op("v", lambda e: e.tensor_scalar(
                    out=out, in0=a, scalar1=s1, scalar2=None, op0=op0),
                    reads, writes, out=out)
            return K.op("v", lambda e: e.tensor_scalar(
                out=out, in0=a, scalar1=s1, scalar2=s2, op0=op0, op1=op1),
                reads, writes, out=out)

        def vstt(out, in0, scalar, in1, op0, op1, reads=(), writes=()):
            return K.op("v", lambda e: e.scalar_tensor_tensor(
                out=out, in0=in0, scalar=scalar, in1=in1, op0=op0, op1=op1),
                reads, writes, out=out)

        def vcp(out, in_, reads=(), writes=()):
            return K.op("v", lambda e: e.tensor_copy(out=out, in_=in_),
                        reads, writes, out=out)

        def vred(out, in_, reads=(), writes=()):
            return K.op("v", lambda e: e.reduce_sum(out=out, in_=in_,
                                                    axis=mybir.AxisListType.X),
                        reads, writes, out=out)

        def vrec(out, in_, reads=(), writes=()):
            return K.op("v", lambda e: e.reciprocal(out=out, in_=in_),
                        reads, writes)

        def vrecf(out, in_, reads=(), writes=()):
            return K.op("v", lambda e: e.reciprocal_approx_fast(out=out, in_=in_),
                        reads, writes)

        def vscan(out, d0, d1, reads=(), writes=()):
            return K.op("v", lambda e: e.tensor_tensor_scan(
                out=out, data0=d0, data1=d1, initial=0.0,
                op0=ALU.add, op1=ALU.add), reads, writes, out=out)

        def vmemset(ap, val, reads=(), writes=()):
            return K.op("v", lambda e: e.memset(ap, val), reads, writes, out=ap)

        def act(out, in_, f, bias=None, scale=None, reads=(), writes=()):
            kw = {}
            if bias is not None:
                kw["bias"] = bias
            if scale is not None:
                kw["scale"] = scale
            return K.op("s", lambda e: e.activation(out=out, in_=in_, func=f, **kw),
                        tuple(reads) + (CONSTS,), writes, out=out)

        def scp(out, in_, reads=(), writes=()):
            return K.op("s", lambda e: e.copy(out=out, in_=in_), reads, writes,
                        out=out)

        def ptt(out, a, b, op, reads=(), writes=()):
            return K.op("p", lambda e: e.tensor_tensor(out=out, in0=a, in1=b, op=op),
                        reads, writes, out=out)

        def qkraw_s(sl): return qkraw[:, sl * (CH + 4):(sl + 1) * (CH + 4)]
        def vfmb_s(sl): return vfmb[:, 0:CH + 128]
        def knfm_g(g): return knfm[:, g * (CH + 128):(g + 1) * (CH + 128)]
        def xn_k(k): return xn[:, k * CH:(k + 1) * CH]
        def ktm_w(g, w): return ktm[:, (g * 9 + w) * 128:(g * 9 + w + 1) * 128]
        def vtm_w(g, w): return vtm[:, (g * 9 + w) * 128:(g * 9 + w + 1) * 128]
        def bsn_t(kv, g, j):
            o = ((kv * NG + g) * 9 + j) * 128
            return bsn[:, o:o + 128]
        def land_c(i): return land[:, i * CH:(i + 1) * CH]
        def landb_c(i): return landb[:, (i - 1) * CH:i * CH]
        def pst_g(g): return pst[:, g * CH:(g + 1) * CH]
        def rst_g(g): return rst[:, g * CH:(g + 1) * CH]
        def xst_g(g): return xst[:, g * CH:(g + 1) * CH]
        def onb_g(g): return onb[:, g * CH:(g + 1) * CH]
        def masks_g(g): return masks[:, g * CH:(g + 1) * CH]
        def ebc_g(g): return ebc[:, g * CH:(g + 1) * CH]
        def abcs_s(sl): return abcs[:, sl * CH:(sl + 1) * CH]
        def E_g(g): return inde[:, g * 128:(g + 1) * 128]
        def ind_g(g): return ind[:, g * 16:(g + 1) * 16]
        def tmA(src, g, j):
            return src(g, j // 2)
        def tm_block(src, g, j, hl):
            if hl == j % 2:
                return src(g, j // 2)
            return src(g, 4 + (j + 1) // 2)

        K.cap = []
        for dst, src in ((idf, idf_d), (idb, idb_d), (sel, sel_d), (trirow, trirow_d),
                         (ind, ind_d), (inde, inde_d), (ones1, ones1_d),
                         (epsrow, epsrow_d), (nrmw, nrmw_d), (bbc, bb_d),
                         (dtbc, dtb_d), (fconst, fconst_d),
                         (indl, indl_d), (eloc, eloc_d)):
            dma(dst[:], src, writes=(CONSTS,))
        for i, v in enumerate(FCONST_VALS):
            nc.const_aps.register(F32, v, fconst[:, i:i + 1])
        dma(colsc[0:16, 0:1], alog_d, writes=(rCOL,))
        for g in range(NG):
            dma(lamg[:, g:g + 1], lam_d[g * 128:(g + 1) * 128, :], writes=(CONSTS,))
            dma(wcq[:, g * 4:(g + 1) * 4], wcq_d[g * 128:(g + 1) * 128, :],
                writes=(CONSTS,))
            dma(wck[:, g * 4:(g + 1) * 4], wck_d[g * 128:(g + 1) * 128, :],
                writes=(CONSTS,))
        dma(stg[0:1, 0:D], bf_d, writes=(rSTG,))
        act(colsc[0:16, 1:2], colsc[0:16, 0:1], AF.Exp, reads=(rCOL,), writes=(rCOL,))
        vts(negea[:, :], colsc[0:16, 1:2], -1.0, ALU.mult, reads=(rCOL,),
            writes=(CONSTS,))
        vcp(brow[:, :], stg[0:1, 0:D], reads=(rSTG,), writes=(rBROW,))
        for g in range(NG):
            act(colsc[:, 2:3], lamg[:, g:g + 1], AF.Abs, reads=(CONSTS, rCOL),
                writes=(rCOL,))
            act(colsc[:, 3:4], colsc[:, 2:3], AF.Exp, scale=-1.0, reads=(rCOL,),
                writes=(rCOL,))
            act(colsc[:, 4:5], colsc[:, 3:4], AF.Ln, bias=1.0, reads=(rCOL,),
                writes=(rCOL,))
            vts(colsc[:, 5:6], lamg[:, g:g + 1], 0.0, ALU.max, reads=(CONSTS, rCOL),
                writes=(rCOL,))
            vtt(colsc[:, 6:7], colsc[:, 4:5], colsc[:, 5:6], ALU.add, reads=(rCOL,),
                writes=(rCOL,))
            vts(colsc[:, 7:8], colsc[:, 6:7], 0.25, ALU.add, reads=(rCOL,),
                writes=(rCOL,))
            vts(lamtile[:, g * 128:(g + 1) * 128], idf[:], colsc[:, 7:8], ALU.mult,
                reads=(CONSTS, rCOL), writes=(rLAM,))

        stg2b = stg2[:].bitcast(BF16)
        for wi, wd in enumerate((wq_d, wk_d, wv_d, wo_d)):
            for kk in range(8):
                dma(stg[:, 0:D], wd[kk * 128:(kk + 1) * 128, :], reads=(rSTG,),
                    writes=(rSTG,))
                vcp(stg2b[:, 0:D], stg[:, 0:D], reads=(rSTG,),
                    writes=(rSTG2, rRCQ[0], rRCQ[1], rRCQ[2], rRCQ[3]))
                if wi < 3:
                    ga, gb = (kk * 1024) // 1152, (kk * 1024 + 1023) // 1152
                    vcp(ktm[:, kk * 1024:(kk + 1) * 1024], stg2b[:, 0:D],
                        reads=(rSTG2,), writes=(rKT[ga], rKT[gb]))
                else:
                    dma(wo_s[kk * 128:(kk + 1) * 128, :], stg2b[:, 0:D],
                        reads=(rSTG2,), writes=(rWST,))
            if wi < 3:
                for m in range(8):
                    for kk in range(8):
                        off = kk * 1024 + m * 128
                        vcp(stg2b[:, 1024 + kk * 128:1024 + (kk + 1) * 128],
                            ktm[:, off:off + 128], reads=(rKT[off // 1152],),
                            writes=(rSTG2, rRCQ[0], rRCQ[1], rRCQ[2], rRCQ[3]))
                    dma(wqkvT_s[(wi * 8 + m) * 128:(wi * 8 + m + 1) * 128, :],
                        stg2b[:, 1024:2048], reads=(rSTG2,), writes=(rWST,))
        for kk in range(8):
            dma(stg[:, 0:H], wa_d[kk * 128:(kk + 1) * 128, :], reads=(rSTG,),
                writes=(rSTG,))
            dma(stg[:, H:2 * H], wb_d[kk * 128:(kk + 1) * 128, :], reads=(rSTG,),
                writes=(rSTG,))
            vcp(wab_sb[:, kk * 32:(kk + 1) * 32], stg[:, 0:32], reads=(rSTG,),
                writes=(rWAB,))

        dma(stg[0:4, 0:100], ld_d, writes=(rSTG,))
        dma(stg[0:50, 100:104], lu_d, writes=(rSTG,))
        for kk in range(8):
            dma(stg[:, 104:154], ua_d[kk * 128:(kk + 1) * 128, :], writes=(rSTG,))
            tr(pb[0][0:50, 0:128], stg[0:128, 104:154][:, 0:50], idf[:],
               reads=(rSTG,), writes=(PB[0],))
            vcp(stg[0:50, 154 + kk * 128:154 + (kk + 1) * 128], pb[0][0:50, 0:128],
                reads=(PB[0],), writes=(rSTG,))
        uaT = stg[0:50, 154:154 + D]
        mm(pb[1][0:4, 0:512], stg[0:50, 100:104], uaT[0:50, 0:512], True, True,
           reads=(rSTG,), writes=(PB[1],))
        mm(pb[2][0:4, 0:512], stg[0:50, 100:104], uaT[0:50, 512:1024], True, True,
           reads=(rSTG,), writes=(PB[2],))
        vcp(land[0:4, 0:512], pb[1][0:4, 0:512], reads=(PB[1],), writes=(rLND[0],))
        vcp(land[0:4, 512:1024], pb[2][0:4, 0:512], reads=(PB[2],), writes=(rLND[1],))
        mm(pb[1][0:100, 0:512], stg[0:4, 0:100], land[0:4, 0:512], True, True,
           reads=(rSTG, rLND[0]), writes=(PB[1],))
        mm(pb[2][0:100, 0:512], stg[0:4, 0:100], land[0:4, 512:1024], True, True,
           reads=(rSTG, rLND[1]), writes=(PB[2],))
        vcp(stg2[0:100, 0:512], pb[1][0:100, 0:512], reads=(PB[1],), writes=(rSTG2, rRCQ[0], rRCQ[1], rRCQ[2], rRCQ[3]))
        vcp(stg2[0:100, 512:1024], pb[2][0:100, 0:512], reads=(PB[2],), writes=(rSTG2, rRCQ[0], rRCQ[1], rRCQ[2], rRCQ[3]))
        dma(stg[0:100, 0:D], da_d, reads=(rSTG,), writes=(rSTG,))
        for it in range(8):
            for oh in range(2):
                mm(pb[3][0:128, 0:512], stg[0:100, it * 128:(it + 1) * 128],
                   stg2[0:100, oh * 512:(oh + 1) * 512], True, True,
                   reads=(rSTG, rSTG2), writes=(PB[3],))
                vcp(outsb[:, 0:512], pb[3][0:128, 0:512], reads=(PB[3],),
                    writes=(rSQ2[0],))
                for oi in range(4):
                    ot = oh * 4 + oi
                    dma(stg[:, 1128:1256], wf_d[ot * 128:(ot + 1) * 128,
                                                it * 128:(it + 1) * 128],
                        reads=(rSTG,), writes=(rSTG,))
                    tr(pb[5][0:128, 0:128], stg[:, 1128:1256], idf[:], reads=(rSTG,),
                       writes=(PB[5],))
                    vstt(stg[:].bitcast(BF16)[:, 2560:2688], pb[5][0:128, 0:128], 1.0,
                         outsb[:, oi * 128:(oi + 1) * 128],
                         ALU.mult, ALU.add, reads=(PB[5], rSQ2[0]),
                         writes=(rSTG,))
                    dma(weff_s[it * 128:(it + 1) * 128, ot * 128:(ot + 1) * 128],
                        stg[:].bitcast(BF16)[:, 2560:2688], reads=(rSTG,),
                        writes=(rWST,))

        for kv in range(2):
            for g in range(NG):
                vmemset(bsn_t(kv, g, 0)[:, :], 0.0, writes=(rBSN[kv][g],))
                for j in range(1, 9):
                    vmemset(bsn_t(kv, g, j)[:, :], 0.0, writes=(rBSN[kv][g],))
        vmemset(qhalo[:, :], 0.0, writes=(rHAL,))
        vmemset(khalo[:, :], 0.0, writes=(rHAL,))
        for sl in range(2):
            vmemset(qkraw_s(sl)[:, 0:4], 0.0, writes=(rQK[sl],))
        vmemset(vfmb_s(0)[:, 0:64], 0.0, writes=(rVF[0],))
        vmemset(vfmb_s(0)[:, 64 + CH:128 + CH], 0.0, writes=(rVF[0],))
        for g in range(NG):
            vmemset(knfm_g(g)[:, 0:64], 0.0, writes=(rKN[g],))
            vmemset(knfm_g(g)[:, 64 + CH:128 + CH], 0.0, writes=(rKN[g],))
        vmemset(zeros16[:, :], 0.0, writes=(CONSTS,))
        wdma(wmt[:, 0:1024], wqkvT_s[0:128, :], 0, reads=(rWST,),
             writes=(rWMT[0],))

        for ch in range(NCH):
            t0 = ch * CH
            for tt in range(NT):
                xs = stg[:, 0:D]
                K.wdma(lambda e, d_=xs, s_=x_d[t0 + tt * 128:
                                                t0 + (tt + 1) * 128, :]:
                       e.dma_start(out=d_, in_=s_), 7,
                       (rSTG,), (rSTG, rSTGQ[0], rSTGQ[1]), out=xs, q="s")
                vred(colsc[:, 0:1], xs, reads=(rSTG,), writes=(rCOL,))
                act(stg2[:, 0:D], xs, AF.Square, reads=(rSTG,), writes=(rSTG2, rRCQ[0], rRCQ[1], rRCQ[2], rRCQ[3]))
                vred(colsc[:, 1:2], stg2[:, 0:D], reads=(rSTG2,), writes=(rCOL,))
                vts(colsc[:, 2:3], colsc[:, 0:1], 1.0 / D, ALU.mult, reads=(rCOL,),
                    writes=(rCOL,))
                vts(colsc[:, 3:4], colsc[:, 1:2], 1.0 / D, ALU.mult, reads=(rCOL,),
                    writes=(rCOL,))
                vts(colsc[:, 4:5], colsc[:, 2:3], colsc[:, 2:3], ALU.mult,
                    reads=(rCOL,), writes=(rCOL,))
                vtt(colsc[:, 5:6], colsc[:, 3:4], colsc[:, 4:5], ALU.subtract,
                    reads=(rCOL,), writes=(rCOL,))
                act(colsc[:, 6:7], colsc[:, 5:6], AF.Ln, bias=1e-5, reads=(rCOL,),
                    writes=(rCOL,))
                act(colsc[:, 7:8], colsc[:, 6:7], AF.Exp, scale=-0.5, reads=(rCOL,),
                    writes=(rCOL,))
                vts(stg2[:].bitcast(BF16)[:, D:2 * D], xs, colsc[:, 2:3],
                    ALU.subtract, colsc[:, 7:8], ALU.mult, reads=(rSTG, rCOL),
                    writes=(rSTG2, rRCQ[0], rRCQ[1], rRCQ[2], rRCQ[3]))
                for kk in range(8):
                    bi = 6 + (kk % 2)
                    tr(pbb[bi][:, 0:128],
                       stg2[:].bitcast(BF16)[:, D + kk * 128:D + (kk + 1) * 128],
                       idb[:], reads=(rSTG2,), writes=(PB[bi],))
                    vcp(xn_k(kk)[:, tt * 128:(tt + 1) * 128], pbb[bi][:, 0:128],
                        reads=(PB[bi],), writes=(rXN[kk],))


            for li, off in ((0, 0), (1, 16)):
                ps = pb[0]
                for kk in range(8):
                    mm(ps[0:16, 0:512], wab_sb[:, kk * 32 + off:kk * 32 + off + 16],
                       xn_k(kk)[:, 0:CH], start=(kk == 0), stop=(kk == 7),
                       reads=(rWAB, rXN[kk]), writes=(PB[0],))
                vcp(land_c(li), ps[0:16, 0:512], reads=(PB[0],),
                    writes=(rLND[li],))

            la, lb = land_c(0), land_c(1)
            act(land_c(2), la, AF.Abs, bias=dtbc[:, :], reads=(rLND[0], CONSTS),
                writes=(rLND[2],))
            act(land_c(2), land_c(2), AF.Exp, scale=-1.0, reads=(rLND[2],),
                writes=(rLND[2],))
            act(land_c(2), land_c(2), AF.Ln, bias=1.0, reads=(rLND[2],),
                writes=(rLND[2],))
            vts(land_c(4), la, dtbc[:, :], ALU.add, 0.0, ALU.max,
                reads=(rLND[0], CONSTS), writes=(rLND[4],))
            vtt(land_c(3), land_c(4), land_c(2), ALU.add, reads=(rLND[4], rLND[2]),
                writes=(rLND[3],))
            vts(land_c(3), land_c(3), negea[:, :], ALU.mult, reads=(rLND[3], CONSTS),
                writes=(rLND[3],))
            act(land_c(4), lb, AF.Sigmoid, bias=bbc[:, :], reads=(rLND[1], CONSTS),
                writes=(rLND[4],))
            act(land_c(5), land_c(4), AF.Ln, reads=(rLND[4],), writes=(rLND[5],))
            for j in range(NJ):
                vscan(land_c(6)[:, j * 64:(j + 1) * 64],
                      land_c(3)[:, j * 64:(j + 1) * 64],
                      zeros16[:, :], reads=(rLND[3], CONSTS),
                      writes=(rLND[6],))
            vtt(land_c(7), land_c(6), land_c(5), ALU.subtract,
                reads=(rLND[6], rLND[5]),
                writes=(rLND[7],))
            act(landb_c(1), land_c(6), AF.Exp, reads=(rLND[6],),
                writes=(rLANDB, rAL[0]))
            vcp(landb_c(4)[:, 0:8], landb_c(1)[:, 63:CH:64],
                reads=(rLANDB, rAL[1]), writes=(rLANDB, rAL[0]))
            for j in range(NJ):
                vstt(land_c(0)[:, j * 64:(j + 1) * 64],
                     land_c(7)[:, j * 64:(j + 1) * 64], -1.0,
                     land_c(6)[:, 63 + j * 64:64 + j * 64]
                     .unsqueeze(2).broadcast_to([16, 1, 64]).squeeze(1),
                     ALU.mult, ALU.add, reads=(rLND[7], rLND[6]),
                     writes=(rLND[0],))
            act(land_c(0), land_c(0), AF.Exp, reads=(rLND[0],), writes=(rLND[0],))
            vcp(landb_c(2), land_c(6), reads=(rLND[6],), writes=(rLANDB, rAL[1]))
            vtt(land_c(1), land_c(6), landb_c(2), ALU.subtract,
                reads=(rLND[6], rLANDB), writes=(rLND[1],))
            vcp(landb_c(3), land_c(1), reads=(rLND[1],), writes=(rLANDB, rAL[1]))
            for tt in range(NT):
                tr(pb[4][0:128, 0:16], land_c(7)[0:16, tt * 128:(tt + 1) * 128],
                   idf[0:16, 0:16], reads=(rLND[7],), writes=(PB[4],))
                vcp(csT[:, tt * 16:(tt + 1) * 16], pb[4][0:128, 0:16],
                    reads=(PB[4],), writes=(rCST,))
                tr(pb[4][0:128, 16:32], land_c(0)[0:16, tt * 128:(tt + 1) * 128],
                   idf[0:16, 0:16], reads=(rLND[0],), writes=(PB[4],))
                vcp(facT[:, tt * 16:(tt + 1) * 16], pb[4][0:128, 16:32],
                    reads=(PB[4],), writes=(rFAC,))
            for m in range(4):
                for par in range(2):
                    for hl in range(2):
                        K.wdma(lambda e, dd=csS[hl * 64:(hl + 1) * 64,
                                                par + 2 * m:64:8],
                               ss=csT[par * 64:(par + 1) * 64,
                                      16 * m + hl:16 * (m + 1):2]:
                               e.dma_start(out=dd, in_=ss,
                                           allow_slow_non_contiguous=True),
                               6, (rCST,), (rCSS,), q="s")
            for g in range(NG):
                b = 6 + (g % 2)
                mm(pb[b][:, 0:512], E_g(g), landb_c(2)[0:16, 0:CH], True, False,
                   reads=(CONSTS, rLANDB), writes=(PB[b],))
                mm(pb[b][:, 0:512], E_g(g), landb_c(3)[0:16, 0:CH], False, False,
                   reads=(CONSTS, rLANDB), writes=(PB[b],))
                mm(pb[b][:, 0:512], sel[:], trirow[:], False, True,
                   reads=(CONSTS,), writes=(PB[b],))
                bcv = csS[:, g * 8:(g + 1) * 8].unsqueeze(2).broadcast_to([128, 8, 64])
                vtt(stg2[:, 0:CH], pb[b][:, 0:512], bcv, ALU.subtract,
                    reads=(PB[b], rCSS), writes=(rSTG2, rRCQ[0], rRCQ[1], rRCQ[2], rRCQ[3]))
                act(masks_g(g), stg2[:, 0:CH], AF.Exp, reads=(rSTG2,),
                    writes=(rMSK[g],))
            for g in range(NG):
                b = 6 + (g % 2)
                mm(pb[b][:, 0:512], E_g(g), landb_c(1)[0:16, 0:CH], True, True,
                   reads=(CONSTS, rLANDB), writes=(PB[b],))
                vcp(ebc_g(g), pb[b][:, 0:512], reads=(PB[b],), writes=(rEBC[g],))

            def proj_mt(mt_i, dst, dreg):
                sl = mt_i % 2
                buf = wmt[:, sl * 1024:(sl + 1) * 1024]
                ps = pb[0]
                for kk in range(8):
                    mm(ps[:, 0:512], buf[:, kk * 128:(kk + 1) * 128],
                       xn_k(kk)[:, 0:CH], start=(kk == 0), stop=(kk == 7),
                       reads=(rWMT[sl], rXN[kk]), writes=(PB[0],))
                nx = mt_i + 1
                if not (ch == NCH - 1 and nx >= 24):
                    wdma(wmt[:, (nx % 2) * 1024:(nx % 2 + 1) * 1024],
                         wqkvT_s[(nx % 24) * 128:(nx % 24 + 1) * 128, :], nx % 2,
                         reads=(rWST,), writes=(rWMT[nx % 2],))
                scp(dst, ps[:, 0:512], reads=(PB[0],), writes=(dreg,))

            def conv_block(which, g, sl):
                rb = qkraw_s(sl)
                halo = qhalo if which == "q" else khalo
                wcol = wcq if which == "q" else wck
                vcp(rb[:, 1:4], halo[:, g * 3:(g + 1) * 3], reads=(rHAL,),
                    writes=(rQK[sl],))
                accv = sqb[:, sl * 1024:sl * 1024 + CH]
                vts(accv, rb[:, 1:1 + CH], wcol[:, g * 4:g * 4 + 1], ALU.mult,
                    reads=(rQK[sl], CONSTS), writes=(rSQ2[sl],))
                for i in (1, 2, 3):
                    vstt(accv, rb[:, 1 + i:1 + i + CH],
                         wcol[:, g * 4 + i:g * 4 + i + 1], accv, ALU.mult,
                         ALU.add, reads=(rQK[sl], CONSTS, rSQ2[sl]),
                         writes=(rSQ2[sl],))
                vcp(halo[:, g * 3:(g + 1) * 3], rb[:, 513:516], reads=(rQK[sl],),
                    writes=(rHAL,))
                return accv, rSQ2[sl]

            qsq = {}
            for g in range(NG + 1):
                if g < NG:
                    sl = g % 2
                    proj_mt(g, qkraw_s(sl)[:, 4:4 + CH], rQK[sl])
                    accv, areg = conv_block("q", g, sl)
                    act(pst_g(g), accv, AF.Silu, reads=(areg,), writes=(rP[g],))
                    act(csc[:, sl * CH:sl * CH + CH], pst_g(g), AF.Square,
                        reads=(rP[g],), writes=(rCSC[sl],))
                    qsq[g] = sl
                if g > 0:
                    slp = qsq[g - 1]
                    mm(pb[2][0:16, 0:512], ind_g(g - 1),
                       csc[:, slp * CH:slp * CH + CH],
                       start=(g - 1 == 0), stop=(g - 1 == NG - 1),
                       reads=(CONSTS, rCSC[slp]), writes=(PB[2],))
            act(land_c(2), pb[2][0:16, 0:512], AF.Ln, bias=1e-6, reads=(PB[2],),
                writes=(rLND[2],))
            act(land_c(2), land_c(2), AF.Exp, scale=-0.5, reads=(rLND[2],),
                writes=(rLND[2],))
            vcp(landb_c(2), land_c(2), reads=(rLND[2],), writes=(rLANDB, rAL[1]))
            for g in range(NG):
                b = 6 + (g % 2)
                mm(pb[b][:, 0:512], E_g(g), landb_c(2)[0:16, 0:CH], True, True,
                   reads=(CONSTS, rLANDB), writes=(PB[b],))
                vtt(pst_g(g), pst_g(g), pb[b][:, 0:512], ALU.mult,
                    reads=(rP[g], PB[b]), writes=(rP[g],))
                vcp(rst_g(g), pst_g(g), reads=(rP[g],), writes=(rRS[g],))
            for g in range(NG + 1):
                if g < NG:
                    sl = g % 2
                    proj_mt(8 + g, qkraw_s(sl)[:, 4:4 + CH], rQK[sl])
                    accv, areg = conv_block("k", g, sl)
                    act(knfm_g(g)[:, 64:64 + CH], accv, AF.Silu, reads=(areg,),
                        writes=(rKN[g],))
                    act(csc[:, sl * CH:sl * CH + CH], knfm_g(g)[:, 64:64 + CH],
                        AF.Square, reads=(rKN[g],), writes=(rCSC[sl],))
                if g > 0:
                    slp = (g - 1) % 2
                    mm(pb[3][0:16, 0:512], ind_g(g - 1),
                       csc[:, slp * CH:slp * CH + CH],
                       start=(g - 1 == 0), stop=(g - 1 == NG - 1),
                       reads=(CONSTS, rCSC[slp]), writes=(PB[3],))
            act(land_c(4), pb[3][0:16, 0:512], AF.Ln, bias=1e-6, reads=(PB[3],),
                writes=(rLND[4],))
            act(land_c(4), land_c(4), AF.Exp, scale=-0.5, reads=(rLND[4],),
                writes=(rLND[4],))
            vcp(landb_c(3), land_c(4), reads=(rLND[4],), writes=(rLANDB, rAL[1]))
            for g in range(NG):
                b = 6 + (g % 2)
                mm(pb[b][:, 0:512], E_g(g), landb_c(3)[0:16, 0:CH], True, True,
                   reads=(CONSTS, rLANDB), writes=(PB[b],))
                vtt(knfm_g(g)[:, 64:64 + CH], knfm_g(g)[:, 64:64 + CH],
                    pb[b][:, 0:512], ALU.mult, reads=(rKN[g], PB[b]),
                    writes=(rKN[g],))
            for g in range(NG):
                for wi in range(9):
                    col = 64 + wi * 128 if wi < 4 else (wi - 4) * 128
                    bi = 6 + (wi % 2)
                    tr(pbb[bi][:, 0:128], knfm_g(g)[:, col:col + 128], idb[:],
                       reads=(rKN[g],), writes=(PB[bi],))
                    vcp(ktm_w(g, wi)[:, :], pbb[bi][:, 0:128], reads=(PB[bi],),
                        writes=(rKT[g],))
            for g in range(NG):
                sl = 0
                proj_mt(16 + g, vfmb_s(sl)[:, 64:64 + CH], rVF[sl])
                for wi in range(9):
                    col = 64 + wi * 128 if wi < 4 else (wi - 4) * 128
                    bi = 6 + (wi % 2)
                    tr(pbb[bi][:, 0:128], vfmb_s(0)[:, col:col + 128], idb[:],
                       reads=(rVF[0],), writes=(PB[bi],))
                    vcp(vtm_w(g, wi)[:, :], pbb[bi][:, 0:128], reads=(PB[bi],),
                        writes=(rVT[g],))
            for g in range(NG):
                vmemset(xst_g(g), 0.0, writes=(rX[g],))

            for g in range(NG):
                s_ = g % 2
                mm(pb[6 + s_][:, 0:8], E_g(g), landb_c(4)[0:16, 0:8], True, True,
                   reads=(CONSTS, rLANDB), writes=(PB[6 + s_],))
                vcp(csS[:, g * 8:(g + 1) * 8], pb[6 + s_][:, 0:8],
                    reads=(PB[6 + s_],), writes=(rCSS,))
            for j in range(NJ):
                ja, jb_ = j // 2, (j % 2) * 64
                for g in range(NG):
                    s_ = g % 4
                    for hl in range(2):
                        vts(khsc[jb_:jb_ + 64, s_ * 128 + hl * 64:
                                 s_ * 128 + (hl + 1) * 64],
                            ktm_w(g, ja)[jb_:jb_ + 64, hl * 64:(hl + 1) * 64],
                            facT[jb_:jb_ + 64, ja * 16 + 2 * g + hl:
                                 ja * 16 + 2 * g + hl + 1], ALU.mult,
                            reads=(rKT[g], rFAC), writes=(rKH[s_],))
                    for kv in range(2):
                        pgx = pb[2 * s_ + kv]
                        other = ktm_w if kv == 0 else vtm_w
                        oreg = rKT[g] if kv == 0 else rVT[g]
                        for hl in range(2):
                            mm(pgx[hl * 64:hl * 64 + 64, hl * 64:(hl + 1) * 64],
                               khsc[jb_:jb_ + 64, s_ * 128 + hl * 64:
                                    s_ * 128 + (hl + 1) * 64],
                               other(g, ja)[jb_:jb_ + 64, hl * 64:(hl + 1) * 64],
                               True, True, reads=(rKH[s_], oreg),
                               writes=(PB[2 * s_ + kv],), tp=(jb_, hl * 64))
                    for kv in range(2):
                        pgx = pb[2 * s_ + kv]
                        for hl in range(2):
                            vstt(bsn_t(kv, g, j + 1)[hl * 64:hl * 64 + 64,
                                                     hl * 64:(hl + 1) * 64],
                                 bsn_t(kv, g, j)[hl * 64:hl * 64 + 64,
                                                 hl * 64:(hl + 1) * 64],
                                 csS[hl * 64:hl * 64 + 64, g * 8 + j:g * 8 + j + 1],
                                 pgx[hl * 64:hl * 64 + 64, hl * 64:(hl + 1) * 64],
                                 ALU.mult, ALU.add,
                                 reads=(rBSN[kv][g], rCSS, PB[2 * s_ + kv]),
                                 writes=(rBSN[kv][g],))

            rcin = stg2[0:4, 0:1024]

            def al(s_):
                return landb[0:4, s_ * 1024:(s_ + 1) * 1024]

            def pr_sl(t, u):
                return t[:, 2 * u * CH:(2 * u + 2) * CH]

            lrs = lambda u: land[0:4, 2 * u * CH:(2 * u + 2) * CH]
            q1ctr = [0]

            def q1slot():
                q1ctr[0] += 1
                return q1ctr[0] % 2

            def q1s(sl):
                return q1b[:, sl * 1024:(sl + 1) * 1024]

            for u in range(4):
                g0 = 2 * u
                qsl = q1slot()
                vtt(q1s(qsl), pr_sl(rst, u), pr_sl(rst, u), ALU.mult,
                    reads=(rRS[g0], rRS[g0 + 1]), writes=(rQ1[qsl],))
                for h in range(2):
                    mm(pq[3][0:4, h * 512:h * 512 + 512], ones1[0:1, 0:4],
                       epsrow[:], True, False, reads=(CONSTS,),
                       writes=(PB[6 + h],))
                    mm(pq[3][0:4, h * 512:h * 512 + 512],
                       indl[:, h * 4:h * 4 + 4],
                       q1s(qsl)[:, h * 512:h * 512 + 512], False, True,
                       reads=(CONSTS, rQ1[qsl]), writes=(PB[6 + h],))
                vcp(lrs(u), pq[3][0:4, 0:1024], reads=(PB[6], PB[7]),
                    writes=(rLND[2 * u], rLND[2 * u + 1]))

            def M_pe1(u, kv):
                s_ = u % 2
                psap = pq[s_]
                bapr = (PB[2 * s_], PB[2 * s_ + 1])
                g0 = 2 * u
                srcT = pst if kv == 0 else xst
                sregs = (rP[g0], rP[g0 + 1]) if kv == 0 else (rX[g0], rX[g0 + 1])
                vtt(ptl[:, 0:1024], pr_sl(srcT, u), pr_sl(ebc, u), ALU.mult,
                    reads=sregs + (rEBC[g0], rEBC[g0 + 1]), writes=(rPTL[0],))
                for h in range(2):
                    g = g0 + h
                    for j in range(NJ):
                        mm(psap[:, h * 512 + j * 64:h * 512 + (j + 1) * 64],
                           bsn_t(kv, g, j)[:, :],
                           ptl[:, h * 512 + j * 64:h * 512 + (j + 1) * 64],
                           start=(j == 0), stop=False,
                           reads=(rBSN[kv][g], rPTL[0]), writes=(bapr[h],))
                    if kv == 0:
                        mm(psap[:, h * 512:h * 512 + 512],
                           lamtile[:, g * 128:(g + 1) * 128],
                           pr_sl(srcT, u)[:, h * 512:h * 512 + 512],
                           False, False, reads=(rLAM,) + sregs, writes=(bapr[h],))
                    for j in range(NJ):
                        for hl in range(2):
                            mm(pq[2][hl * 64:hl * 64 + 64,
                                     h * 512 + j * 64:h * 512 + (j + 1) * 64],
                               knfm_g(g)[hl * 64:hl * 64 + 64,
                                         64 + j * 64:64 + (j + 1) * 64],
                               srcT[hl * 64:hl * 64 + 64,
                                    g * CH + j * 64:g * CH + (j + 1) * 64],
                               True, True, reads=(rKN[g],) + sregs,
                               writes=(PB[4 + h],), tp=(hl * 64, hl * 64))

            def M_mask(u, kv):
                g0 = 2 * u
                vtt(csc[:, 0:1024], pq[2][:, 0:1024],
                    masks[:, g0 * CH:(g0 + 2) * CH], ALU.mult,
                    reads=(PB[4], PB[5], rMSK[g0], rMSK[g0 + 1]),
                    writes=(rCSC[0], rCSC[1]))

            def M_recon(u, kv):
                s_ = u % 2
                psap = pq[s_]
                bapr = (PB[2 * s_], PB[2 * s_ + 1])
                g0 = 2 * u
                wtm = ktm_w if kv == 0 else vtm_w
                for h in range(2):
                    g = g0 + h
                    wreg = (rKT[g] if kv == 0 else rVT[g])
                    for j in range(NJ):
                        for hl in range(2):
                            tile = tm_block(wtm, g, j, hl)
                            mm(psap[hl * 64:hl * 64 + 64,
                                    h * 512 + j * 64:h * 512 + (j + 1) * 64],
                               tile[hl * 64:hl * 64 + 64, hl * 64:(hl + 1) * 64],
                               csc[hl * 64:hl * 64 + 64,
                                   h * 512 + j * 64:h * 512 + (j + 1) * 64],
                               start=False, stop=(j == NJ - 1 and hl == 1),
                               reads=(wreg, rCSC[0], rCSC[1]), writes=(bapr[h],),
                               tp=(hl * 64, hl * 64))

            def M_pap(u):
                s_ = u % 2
                psap = pq[s_]
                bapr = (PB[2 * s_], PB[2 * s_ + 1])
                g0 = 2 * u
                qsl = q1slot()
                vtt(q1s(qsl), pr_sl(pst, u), psap[:, 0:1024], ALU.mult,
                    reads=(rP[g0], rP[g0 + 1]) + bapr, writes=(rQ1[qsl],))
                for h in range(2):
                    mm(pq[3][0:4, h * 512:h * 512 + 512],
                       indl[:, h * 4:h * 4 + 4],
                       q1s(qsl)[:, h * 512:h * 512 + 512], True, True,
                       reads=(CONSTS, rQ1[qsl]), writes=(PB[6 + h],))

            def A_alpha(u):
                s_ = u % 2
                act(rcin, pq[3][0:4, 0:1024], AF.Ln, bias=1e-12,
                    reads=(PB[6], PB[7]), writes=(rRCQ[0],))
                act(rcin, rcin, AF.Exp, scale=-1.0, reads=(rRCQ[0],),
                    writes=(rRCQ[0],))
                vtt(al(s_), lrs(u), rcin, ALU.mult,
                    reads=(rLND[2 * u], rLND[2 * u + 1], rRCQ[0]),
                    writes=(rAL[s_],))

            def B_bcast(u):
                s_ = u % 2
                g0 = 2 * u
                for h in range(2):
                    mm(pq[2][:, h * 512:h * 512 + 512],
                       eloc[0:4, h * 128:h * 128 + 128],
                       al(s_)[0:4, h * 512:h * 512 + 512], True, True,
                       reads=(CONSTS, rAL[s_]), writes=(PB[4 + h],))
                scp(abcs[:, 0:1024], pq[2][:, 0:1024], reads=(PB[4], PB[5]),
                    writes=(rABC[0],))

            def C_upd(u):
                s_ = u % 2
                psap = pq[s_]
                bapr = (PB[2 * s_], PB[2 * s_ + 1])
                g0 = 2 * u
                vtt(ptl[:, 0:1024], abcs[:, 0:1024], psap[:, 0:1024], ALU.mult,
                    reads=(rABC[0],) + bapr, writes=(rPTL[0],))
                vtt(pr_sl(rst, u), pr_sl(rst, u), ptl[:, 0:1024], ALU.subtract,
                    reads=(rRS[g0], rRS[g0 + 1], rPTL[0]),
                    writes=(rRS[g0], rRS[g0 + 1]))
                vtt(sqb[:, s_ * 1024:(s_ + 1) * 1024], pr_sl(rst, u),
                    pr_sl(rst, u), ALU.mult,
                    reads=(rRS[g0], rRS[g0 + 1]), writes=(rSQ2[s_],))
                qsl = q1slot()
                vtt(q1s(qsl), abcs[:, 0:1024], pr_sl(pst, u), ALU.mult,
                    reads=(rABC[0], rP[g0], rP[g0 + 1]), writes=(rQ1[qsl],))
                ptt(pr_sl(xst, u), pr_sl(xst, u), q1s(qsl), ALU.add,
                    reads=(rX[g0], rX[g0 + 1], rQ1[qsl]),
                    writes=(rX[g0], rX[g0 + 1]))

            def D_rs(u):
                s_ = u % 2
                g0 = 2 * u
                for h in range(2):
                    mm(pq[3][0:4, h * 512:h * 512 + 512], ones1[0:1, 0:4],
                       epsrow[:], True, False, reads=(CONSTS,),
                       writes=(PB[6 + h],))
                    mm(pq[3][0:4, h * 512:h * 512 + 512],
                       indl[:, h * 4:h * 4 + 4],
                       sqb[:, s_ * 1024 + h * 512:s_ * 1024 + h * 512 + 512],
                       False, True, reads=(CONSTS, rSQ2[s_]), writes=(PB[6 + h],))
                act(rcin, lrs(u), AF.Ln, reads=(rLND[2 * u], rLND[2 * u + 1]),
                    writes=(rRCQ[0],))
                act(rcin, rcin, AF.Exp, scale=-1.0, reads=(rRCQ[0],),
                    writes=(rRCQ[0],))

            def E_beta(u):
                s_ = u % 2
                g0 = 2 * u
                vtt(al(s_), pq[3][0:4, 0:1024], rcin, ALU.mult,
                    reads=(PB[6], PB[7], rRCQ[0]), writes=(rAL[s_],))
                vcp(lrs(u), pq[3][0:4, 0:1024], reads=(PB[6], PB[7]),
                    writes=(rLND[2 * u], rLND[2 * u + 1]))
                for h in range(2):
                    mm(pq[2][:, h * 512:h * 512 + 512],
                       eloc[0:4, h * 128:h * 128 + 128],
                       al(s_)[0:4, h * 512:h * 512 + 512], True, True,
                       reads=(CONSTS, rAL[s_]), writes=(PB[4 + h],))
                scp(abcs[:, 0:1024], pq[2][:, 0:1024], reads=(PB[4], PB[5]),
                    writes=(rABC[0],))
                vtt(ptl[:, 0:1024], abcs[:, 0:1024], pr_sl(pst, u), ALU.mult,
                    reads=(rABC[0], rP[g0], rP[g0 + 1]), writes=(rPTL[0],))
                vtt(pr_sl(pst, u), pr_sl(rst, u), ptl[:, 0:1024], ALU.add,
                    reads=(rRS[g0], rRS[g0 + 1], rPTL[0]),
                    writes=(rP[g0], rP[g0 + 1]))

            units = [u for it in range(CG_ITERS) for u in range(4)]
            pu = None
            for u in units:
                if pu is not None:
                    B_bcast(pu)
                M_pe1(u, 0)
                if pu is not None:
                    C_upd(pu)
                M_mask(u, 0)
                M_recon(u, 0)
                if pu is not None:
                    D_rs(pu)
                    E_beta(pu)
                M_pap(u)
                A_alpha(u)
                pu = u
            B_bcast(pu)
            C_upd(pu)
            D_rs(pu)
            E_beta(pu)
            pu = None
            for u in range(4):
                M_pe1(u, 1)
                M_mask(u, 1)
                M_recon(u, 1)
                s_ = u % 2
                psap = pq[s_]
                bapr = (PB[2 * s_], PB[2 * s_ + 1])
                g0 = 2 * u
                act(q1b[:, 0:1024], psap[:, 0:1024], AF.Square, reads=bapr,
                    writes=(rQ1[0],))
                for h in range(2):
                    mm(pq[3][0:16, 0:512], ind_g(g0 + h),
                       q1b[:, h * 512:h * 512 + 512], start=(g0 + h == 0),
                       stop=(g0 + h == NG - 1), reads=(CONSTS, rQ1[0]),
                       writes=(PB[6],))
                scp(pr_sl(onb, u), psap[:, 0:1024], reads=bapr,
                    writes=(rON[g0], rON[g0 + 1]))
            vts(land_c(2), pq[3][0:16, 0:512], 1.0 / 64, ALU.mult, reads=(PB[6],),
                writes=(rLND[2],))
            act(land_c(2), land_c(2), AF.Ln, bias=1e-5, reads=(rLND[2],),
                writes=(rLND[2],))
            act(land_c(2), land_c(2), AF.Exp, scale=-0.5, reads=(rLND[2],),
                writes=(rLND[2],))
            vcp(landb_c(2), land_c(2), reads=(rLND[2],), writes=(rLANDB, rAL[1]))
            for g in range(NG):
                b = 4 + (g % 2)
                mm(pb[b][:, 0:512], E_g(g), landb_c(2)[0:16, 0:CH], True, True,
                   reads=(CONSTS, rLANDB), writes=(PB[b],))
                vstt(onb_g(g), onb_g(g), nrmw[:, :], pb[b][:, 0:512], ALU.mult,
                     ALU.mult, reads=(rON[g], CONSTS, PB[b]), writes=(rON[g],))

            def wrl(i):
                src = wo_s if i < 8 else weff_s
                r = i % 8
                wdma(wrhs[:, (i % 3) * 1024:(i % 3 + 1) * 1024],
                     src[r * 128:(r + 1) * 128, :], 2 + (i % 3),
                     reads=(rWST,), writes=(rWRHS[i % 3],))
            for i in range(3):
                wrl(i)
            for i in range(16):
                sl = i % 3
                for tt in range(NT):
                    for mh in range(2):
                        mm(pb[tt * 2 + mh][:, 0:512],
                           (onb_g(i) if i < 8 else xn_k(i - 8))
                           [:, tt * 128:(tt + 1) * 128],
                           wrhs[:, sl * 1024 + mh * 512:sl * 1024 + (mh + 1) * 512],
                           start=(i == 0), stop=False,
                           reads=((rON[i] if i < 8 else rXN[i - 8]), rWRHS[sl]),
                           writes=(PB[tt * 2 + mh],))
                if i + 3 < 16:
                    wrl(i + 3)
            for mh in range(2):
                for tt in range(NT):
                    mm(pb[tt * 2 + mh][:, 0:512], ones1[0:1, 0:128],
                       brow[0:1, mh * 512:(mh + 1) * 512], start=False, stop=True,
                       reads=(CONSTS, rBROW), writes=(PB[tt * 2 + mh],))
                    wdma(stg[:, (tt % 2) * CH:(tt % 2 + 1) * CH],
                         x_d[t0 + tt * 128:t0 + (tt + 1) * 128,
                             mh * 512:(mh + 1) * 512], 10 + tt % 2,
                         reads=(),
                         writes=(rSTG, rSTGQ[tt % 2]))
                    osl = (mh * NT + tt) % 2
                    vtt(sqbf[:, osl * CH:(osl + 1) * CH],
                        pb[tt * 2 + mh][:, 0:512],
                        stg[:, (tt % 2) * CH:(tt % 2 + 1) * CH],
                        ALU.add, reads=(PB[tt * 2 + mh], rSTGQ[tt % 2]),
                        writes=(rSQ2[osl],))
                    wdma(out_d[t0 + tt * 128:t0 + (tt + 1) * 128,
                               mh * 512:(mh + 1) * 512],
                         sqbf[:, osl * CH:(osl + 1) * CH], 8 + osl,
                         reads=(rSQ2[osl],), writes=())

            for kv in range(2):
                for g in range(NG):
                    vcp(bsn_t(kv, g, 0)[:, :], bsn_t(kv, g, 8)[:, :],
                        reads=(rBSN[kv][g],), writes=(rBSN[kv][g],))

        recs = K.cap
        K.cap = None
        order = _schedule(recs) if SCHEDULE else range(len(recs))
        for idx in order:
            e, slot, fn, reads, writes, out = recs[idx]
            if e.startswith("w"):
                K.wdma(fn, slot, reads, writes, q=e[1])
            else:
                K.op(e, fn, reads, writes)

        K.q["d"].append(lambda e, v=K.cnt["d"]: e.wait_ge(sems["d"], v))
        for osl in range(2):
            K.q["d"].append(lambda e, sm=sems[f"w{8 + osl}"],
                            v=K.wcnt[8 + osl]: e.wait_ge(sm, v))
        K.emit(blk)
    return nc


def _consts():
    idf = np.eye(128, dtype=np.float32)
    sel = np.zeros((64, 128), np.float32)
    for k in range(64):
        sel[k, k] = 1.0
        sel[k, 64 + k] = 1.0
    tri = np.zeros((64, CH), np.float32)
    for sl in range(64):
        for j in range(NJ):
            tri[sl, j * 64:j * 64 + sl] = -BIGNEG
    tri = np.zeros((64, CH), np.float32)
    for sl in range(64):
        for j in range(NJ):
            tri[sl, j * 64:j * 64 + sl] = BIGNEG
    ind = np.zeros((128, 128), np.float32)
    inde = np.zeros((16, NG * 128), np.float32)
    for g in range(NG):
        for p in range(128):
            ind[p, g * 16 + 2 * g + p // 64] = 1.0
            inde[2 * g + p // 64, g * 128 + p] = 1.0
    ones1 = np.ones((1, 128), np.float32)
    epsrow = np.full((1, CH), 1e-12, np.float32)
    indl = np.zeros((128, 8), np.float32)
    eloc = np.zeros((4, 256), np.float32)
    for p in range(128):
        indl[p, 0 + p // 64] = 1.0
        indl[p, 4 + 2 + p // 64] = 1.0
        eloc[0 + p // 64, p] = 1.0
        eloc[2 + p // 64, 128 + p] = 1.0
    return dict(IDF=idf, IDB=np.eye(128).astype(bfloat16),
                SEL=sel.astype(bfloat16), TRIROW=tri.astype(bfloat16),
                IND=ind.astype(bfloat16), INDE=inde.astype(bfloat16),
                ONES1=ones1.astype(bfloat16), EPSROW=epsrow.astype(bfloat16),
                FCONST=np.tile(np.array([FCONST_VALS], np.float32), (128, 1)),
                INDL=indl.astype(bfloat16), ELOC=eloc.astype(bfloat16))


def _in_map(core, inputs, T):
    c = _consts()
    hs = np.ascontiguousarray(np.asarray(inputs["L0_attn_hidden_states"],
                                         np.float32)[core, :T])
    return dict(
        x=hs,
        Wq=np.asarray(inputs["L0_attn_Wq"], np.float32),
        Wk=np.asarray(inputs["L0_attn_Wk"], np.float32),
        Wv=np.asarray(inputs["L0_attn_Wv"], np.float32),
        Wa=np.asarray(inputs["L0_attn_Wa"], np.float32),
        Wb=np.asarray(inputs["L0_attn_Wb"], np.float32),
        bb=np.asarray(inputs["L0_attn_bb"], np.float32).reshape(H, 1),
        A_log=np.asarray(inputs["L0_attn_A_log"], np.float32).reshape(H, 1),
        dt_bias=np.asarray(inputs["L0_attn_dt_bias"], np.float32).reshape(H, 1),
        lambda_params=np.asarray(inputs["L0_attn_lambda_params"],
                                 np.float32).reshape(D, 1),
        Wconv_q=np.asarray(inputs["L0_attn_Wconv_q"], np.float32),
        Wconv_k=np.asarray(inputs["L0_attn_Wconv_k"], np.float32),
        norm_w=np.tile(np.asarray(inputs["L0_attn_norm_w"], np.float32),
                       2).reshape(128, 1),
        Wo=np.asarray(inputs["L0_attn_Wo"], np.float32),
        W_ffn=np.asarray(inputs["L0_ffn_W"], np.float32),
        b_ffn=np.asarray(inputs["L0_ffn_b"], np.float32).reshape(1, D),
        lora_down=np.asarray(inputs["L0_ffn_lora_down"], np.float32),
        lora_up=np.asarray(inputs["L0_ffn_lora_up"], np.float32),
        down_aux=np.asarray(inputs["L0_ffn_down_aux"], np.float32),
        up_aux=np.asarray(inputs["L0_ffn_up_aux"], np.float32),
        **c)


def kernel(**inputs):
    T = 4096
    nc = build_nc(T)
    in_maps = [_in_map(i, inputs, T) for i in range(NCORES)]
    res = run_bass_kernel_spmd(nc, in_maps, core_ids=list(range(NCORES)))
    out = np.stack([np.asarray(res.results[i]["out"], np.float32)
                    for i in range(NCORES)], axis=0)
    return out


# version 33 (final)
# speedup vs baseline: 1.0082x; 1.0082x over previous
import contextlib
import heapq
import os
import numpy as np
from ml_dtypes import bfloat16

import concourse.bass as bass
import concourse.mybir as mybir
from concourse.bass_utils import run_bass_kernel_spmd

F32 = mybir.dt.float32
BF16 = mybir.dt.bfloat16
AF = mybir.ActivationFunctionType
ALU = mybir.AluOpType

NCORES = 8
D = 1024
H = 16
NG = 8
CH = 512
NJ = 8
NT = 4
CG_ITERS = 15
BIGNEG = -30000.0
FCONST_VALS = (1e-5, 1.0, 1e-6, -1.0, -0.5, 0.0, 0.5, 1e-12)


class KB:

    ENGS = ("v", "s", "t", "p", "d")

    def __init__(self, nc):
        self.nc = nc
        self.q = {e: [] for e in self.ENGS}
        self.cnt = dict.fromkeys(self.ENGS, 0)
        self.hw = {}
        self.sem = {}
        self.reg = {}
        self.wcnt = [0] * 12
        self.cap = None

    def region(self, name, psum=False):
        self.reg[name] = {"w": None, "r": [], "psum": psum}
        return name

    def _wait(self, e, mark):
        if mark is None:
            return
        pe, val = mark
        if pe == e and val <= 0:
            return
        if pe == "d":
            val = self.cnt["d"]
        key = (e, pe)
        if self.hw.get(key, 0) >= val:
            return
        self.hw[key] = val
        sem = self.sem[pe]
        self.q[e].append(lambda eng, s=sem, v=val: eng.wait_ge(s, v))

    def op(self, e, fn, reads=(), writes=(), extra=(), out=None):
        if self.cap is not None:
            self.cap.append((e, None, fn, tuple(reads), tuple(writes), out))
            return None
        for m in extra:
            self._wait(e, m)
        for rn in reads:
            r = self.reg[rn]
            self._wait(e, r["w"])
        for wn in writes:
            r = self.reg[wn]
            wm = r["w"]
            if wm is not None and not (r["psum"] and e == "t" and wm[0] == "t"):
                self._wait(e, wm)
            for m in r["r"]:
                if r["psum"] and e == "t" and m[0] == "t":
                    continue
                self._wait(e, m)
        step = 16 if e == "d" else 1
        self.cnt[e] += step
        mark = (e, self.cnt[e])
        sem = self.sem[e]
        self.q[e].append(lambda eng, f=fn, s=sem, n=step: f(eng).then_inc(s, n))
        for rn in reads:
            self.reg[rn]["r"].append(mark)
        for wn in writes:
            r = self.reg[wn]
            r["w"] = mark
            r["r"] = []
        return mark

    def wdma(self, fn, slot, reads=(), writes=(), out=None, q="d"):
        if self.cap is not None:
            self.cap.append(("w" + q, slot, fn, tuple(reads), tuple(writes),
                             out))
            return None
        e = q
        for rn in reads:
            self._wait(e, self.reg[rn]["w"])
        for wn in writes:
            r = self.reg[wn]
            if r["w"] is not None:
                self._wait(e, r["w"])
            for m in r["r"]:
                self._wait(e, m)
        key = "w%d" % slot
        self.wcnt[slot] += 16
        mark = (key, self.wcnt[slot])
        sem = self.sem[key]
        self.q[e].append(lambda eng, f=fn, s=sem: f(eng).then_inc(s, 16))
        for rn in reads:
            self.reg[rn]["r"].append(mark)
        for wn in writes:
            r = self.reg[wn]
            r["w"] = mark
            r["r"] = []
        return mark

    def emit(self, blk):
        for e, sec in (("d", blk.sync), ("v", blk.vector), ("s", blk.scalar),
                       ("t", blk.tensor), ("p", blk.gpsimd)):
            ops = self.q[e]

            def body(eng, ops=ops):
                for f in ops:
                    f(eng)
            sec(body)


SCHEDULE = True
_SCHED_WIN = float(os.environ.get("SCHED_WIN", "0.01"))
_SCHED_HOP = float(os.environ.get("SCHED_HOP", "0.1"))
_SCHED_TK = float(os.environ.get("SCHED_TK", "0.03"))
_SCHED_TC = float(os.environ.get("SCHED_TC", "0.0008"))


def _schedule(recs):
    n = len(recs)
    eng = [None] * n
    dur = [0.0] * n
    flight = [0.0] * n
    isdma = [False] * n
    for i, (e, slot, fn, reads, writes, out) in enumerate(recs):
        ee = e[1] if e.startswith("w") else e
        eng[i] = ee
        cols, nbytes = 512, 262144
        if out is not None:
            try:
                sh = out.shape
                rows = sh[0]
                cols = 1
                for s in sh[1:]:
                    cols *= s
                nbytes = rows * cols * (2 if "bf" in str(out.dtype) else 4)
            except Exception:
                pass
        if ee == "v":
            dur[i] = 0.10 + cols * 0.00095
        elif ee == "s":
            if e == "ws":
                dur[i] = 0.30
                flight[i] = 1.5 + nbytes * 2e-5
            else:
                dur[i] = 0.18 + cols * 0.00095
        elif ee == "p":
            dur[i] = 0.30 + cols * 0.0018
        elif ee == "t":
            dur[i] = _SCHED_TK + cols * _SCHED_TC
        else:
            dur[i] = 0.30
            flight[i] = 1.5 + nbytes * 2e-5
            isdma[i] = e == "d"
    preds = [set() for _ in range(n)]
    succs = [set() for _ in range(n)]
    lastw = {}
    readers = {}
    for i, (e, slot, fn, reads, writes, out) in enumerate(recs):
        for rn in reads:
            j = lastw.get(rn)
            if j is not None and j != i:
                preds[i].add(j)
                succs[j].add(i)
        for wn in writes:
            j = lastw.get(wn)
            if j is not None and j != i:
                preds[i].add(j)
                succs[j].add(i)
            for rd in readers.get(wn, ()):
                if rd != i:
                    preds[i].add(rd)
                    succs[rd].add(i)
        for rn in reads:
            readers.setdefault(rn, []).append(i)
        for wn in writes:
            lastw[wn] = i
            readers[wn] = []
    prio = [0.0] * n
    for i in range(n - 1, -1, -1):
        m = 0.0
        for k in succs[i]:
            if prio[k] > m:
                m = prio[k]
        prio[i] = m + dur[i]
    hasdp = [any(isdma[j] for j in preds[i]) for i in range(n)]
    nleft = [len(preds[i]) for i in range(n)]
    dready = [0.0] * n
    free = dict.fromkeys("vstpd", 0.0)
    heaps = {e: [] for e in "vstpd"}
    for i in range(n):
        if nleft[i] == 0:
            heapq.heappush(heaps[eng[i]], (0.0, -prio[i], i))
    dmax = 0.0
    order = []
    for _ in range(n):
        best = None
        for e in "vstpd":
            h = heaps[e]
            if not h:
                continue
            t0 = max(free[e], h[0][0])
            tmp = []
            cand = None
            while h and h[0][0] <= t0 + _SCHED_WIN:
                it = heapq.heappop(h)
                tmp.append(it)
                if cand is None or it[1] < cand[1]:
                    cand = it
            for it in tmp:
                if it is not cand:
                    heapq.heappush(h, it)
            st = max(free[e], cand[0])
            if best is None or (st, cand[1]) < (best[0], best[1][1]):
                if best is not None:
                    heapq.heappush(heaps[best[2]], best[1])
                best = (st, cand, e)
            else:
                heapq.heappush(heaps[e], cand)
        st, (dr, nprio, i), e = best
        if hasdp[i]:
            st = max(st, dmax)
        fin = st + dur[i]
        free[e] = fin
        comp = fin + flight[i]
        if isdma[i]:
            dmax = max(dmax, comp)
        order.append(i)
        for k in succs[i]:
            lat = 0.0 if (eng[i] == eng[k] or eng[i] == "d") else _SCHED_HOP
            if comp + lat > dready[k]:
                dready[k] = comp + lat
            nleft[k] -= 1
            if nleft[k] == 0:
                heapq.heappush(heaps[eng[k]], (dready[k], -prio[k], k))
    _schedule.stats = {"makespan": max(free.values()),
                       "busy": {e: round(sum(dur[i] for i in range(n)
                                             if eng[i] == e), 1)
                                for e in "vstpd"}}
    return order


def build_nc(T=4096):
    NCH = T // CH
    nc = bass.Bass("TRN2", target_bir_lowering=False)
    K = KB(nc)
    R = K.region

    def din(name, shape, dt=F32):
        return nc.dram_tensor(name, shape, dt, kind="ExternalInput").ap()

    x_d = din("x", [T, D])
    wq_d = din("Wq", [D, D]); wk_d = din("Wk", [D, D]); wv_d = din("Wv", [D, D])
    wa_d = din("Wa", [D, H]); wb_d = din("Wb", [D, H])
    bb_d = din("bb", [H, 1]); alog_d = din("A_log", [H, 1]); dtb_d = din("dt_bias", [H, 1])
    lam_d = din("lambda_params", [D, 1])
    wcq_d = din("Wconv_q", [D, 4]); wck_d = din("Wconv_k", [D, 4])
    nrmw_d = din("norm_w", [128, 1])
    wo_d = din("Wo", [D, D])
    wf_d = din("W_ffn", [D, D]); bf_d = din("b_ffn", [1, D])
    ld_d = din("lora_down", [4, 100]); lu_d = din("lora_up", [50, 4])
    da_d = din("down_aux", [100, D]); ua_d = din("up_aux", [D, 50])
    idf_d = din("IDF", [128, 128])
    idb_d = din("IDB", [128, 128], BF16)
    sel_d = din("SEL", [64, 128], BF16)
    trirow_d = din("TRIROW", [64, CH], BF16)
    ind_d = din("IND", [128, 128], BF16)
    inde_d = din("INDE", [16, NG * 128], BF16)
    ones1_d = din("ONES1", [1, 128], BF16)
    epsrow_d = din("EPSROW", [1, CH], BF16)
    indl_d = din("INDL", [128, 8], BF16)
    eloc_d = din("ELOC", [4, 256], BF16)
    fconst_d = din("FCONST", [128, 8])
    out_d = nc.dram_tensor("out", [T, D], F32, kind="ExternalOutput").ap()
    wqkvT_s = nc.dram_tensor("WQKVT_BF", [24 * 128, D], BF16, kind="Internal").ap()
    wo_s = nc.dram_tensor("WO_BF", [D, D], BF16, kind="Internal").ap()
    weff_s = nc.dram_tensor("WEFF_BF", [D, D], BF16, kind="Internal").ap()

    es = contextlib.ExitStack()

    def sb(name, shape, dt=F32):
        return es.enter_context(nc.sbuf_tensor(name, shape, dt))

    with es:
        idf = sb("idf", [128, 128]); idb = sb("idb", [128, 128], BF16)
        sel = sb("sel", [64, 128], BF16); trirow = sb("trirow", [64, CH], BF16)
        ind = sb("ind", [128, 128], BF16); inde = sb("inde", [16, NG * 128], BF16)
        ones1 = sb("ones1", [1, 128], BF16); epsrow = sb("epsrow", [1, CH], BF16)
        indl = sb("indl", [128, 8], BF16); eloc = sb("eloc", [4, 256], BF16)
        fconst = sb("fconst", [128, 8])
        nrmw = sb("nrmw", [128, 1])
        bbc = sb("bbc", [H, 1]); negea = sb("negea", [H, 1]); dtbc = sb("dtbc", [H, 1])
        lamg = sb("lamg", [128, NG])
        lamtile = sb("lamtile", [128, NG * 128], BF16)
        wcq = sb("wcq", [128, NG * 4]); wck = sb("wck", [128, NG * 4])
        brow = sb("brow", [1, D], BF16)
        colsc = sb("colsc", [128, 16])
        wmt = sb("wmt", [128, 2048], BF16)
        wab_sb = sb("wab_sb", [128, 256], BF16)
        wrhs = sb("wrhs", [128, 3 * 1024], BF16)
        stg = sb("stg", [128, 1344])
        stg2 = sb("stg2", [128, D])
        xn = sb("xn", [128, NG * CH], BF16)
        qkraw = sb("qkraw", [128, 2 * (CH + 4)], BF16)
        qhalo = sb("qhalo", [128, NG * 3], BF16)
        khalo = sb("khalo", [128, NG * 3], BF16)
        vfmb = sb("vfmb", [128, CH + 128], BF16)
        sqbf = sb("sqbf", [128, 1024])
        sqb = sqbf[:].bitcast(BF16)
        knfm = sb("knfm", [128, NG * (CH + 128)], BF16)
        ktm = sb("ktm", [128, NG * 9 * 128], BF16)
        vtm = sb("vtm", [128, NG * 9 * 128], BF16)
        khsc = sb("khsc", [128, 256], BF16)
        masks = sb("masks", [128, NG * CH], BF16)
        ebc = sb("ebc", [128, NG * CH], BF16)
        bsn = sb("bsn", [128, 2 * NG * 9 * 128], BF16)
        pst = sb("pst", [128, NG * CH], BF16)
        rst = sb("rst", [128, NG * CH], BF16)
        xst = sb("xst", [128, NG * CH], BF16)
        onb = sb("onb", [128, NG * CH], BF16)
        csc = sb("csc", [128, 2 * CH], BF16)
        ptl = sb("ptl", [128, 2 * CH], BF16)
        q1b = sb("q1b", [128, 4 * CH], BF16)
        abcs = sb("abcs", [128, 2 * CH], BF16)
        land = sb("land", [16, 8 * CH])
        zeros16 = sb("zeros16", [16, 64])
        landb = sb("landb", [16, 4 * CH], BF16)
        outsb = sqbf[:, 0:CH]
        csT = sb("csT", [128, 64])
        facT = sb("facT", [128, 64])
        csS = sb("csS", [128, NG * 8])
        pq = [es.enter_context(nc.psum_tensor(f"pq{i}", [128, 1024], F32))
              for i in range(4)]
        pb = [pq[i // 2][:, (i % 2) * 512:(i % 2) * 512 + 512] for i in range(8)]
        pbb = [p.bitcast(BF16) for p in pb]

        sems = {e: es.enter_context(nc.semaphore(f"sem_{e}")) for e in KB.ENGS}
        for i in range(12):
            sems[f"w{i}"] = es.enter_context(nc.semaphore(f"sem_w{i}"))
        K.sem = sems
        blk = es.enter_context(nc.Block())

        PB = [R(f"pb{i}", psum=True) for i in range(8)]
        CONSTS = R("consts")
        rCOL = R("colsc"); rLANDB = R("landb")
        rLND = [R(f"land{i}") for i in range(8)]
        rCST = R("csT"); rFAC = R("facT"); rCSS = R("csS")
        rRCQ = [R("rcq0"), R("rcq1"), R("rcq2"), R("rcq3")]
        rAL = [R("al0"), R("al1")]
        rSTG = R("stg"); rSTG2 = R("stg2")
        rSTGQ = [R("stgq0"), R("stgq1")]
        rWMT = [R("wmt0"), R("wmt1"), R("wmt2")]
        rWAB = R("wab_sb")
        rWRHS = [R(f"wrhs{i}") for i in range(3)]; rBROW = R("brow")
        rXN = [R(f"xn{k}") for k in range(8)]
        rQK = [R("qkraw0"), R("qkraw1")]
        rSQ2 = [R("sq0"), R("sq1")]
        rHAL = R("halos")
        rVF = [R("vfmb0")]
        rKN = [R(f"knfm{g}") for g in range(NG)]
        rKT = [R(f"ktm{g}") for g in range(NG)]
        rVT = [R(f"vtm{g}") for g in range(NG)]
        rKH = [R("khsc0"), R("khsc1")]
        rMSK = [R(f"masks{g}") for g in range(NG)]
        rEBC = [R(f"ebc{g}") for g in range(NG)]
        rBSN = [[R(f"bsn{kv}_{g}") for g in range(NG)] for kv in range(2)]
        rP = [R(f"pst{g}") for g in range(NG)]
        rRS = [R(f"rst{g}") for g in range(NG)]
        rX = [R(f"xst{g}") for g in range(NG)]
        rON = [R(f"onb{g}") for g in range(NG)]
        rCSC = [R("csc0"), R("csc0x"), R("cscq")]
        rPTL = [R("ptl0"), R("ptl1")]
        rQ1 = [R("q1b0"), R("q1b1")]
        rTR = rPTL
        rTX = rQ1
        rABC = [R("abcs0"), R("abcs1")]
        rLAM = R("lamtile"); rWST = R("wstage")

        def dma(dst, src, reads=(), writes=(), extra=()):
            return K.op("d", lambda e: e.dma_start(out=dst, in_=src),
                        reads, writes, extra, out=dst)

        def wdma(dst, src, slot, reads=(), writes=()):
            return K.wdma(lambda e: e.dma_start(out=dst, in_=src), slot,
                          reads, writes, out=dst)

        def mm(out, lhsT, rhs, start, stop, reads=(), writes=(), tp=None):
            if tp is None:
                return K.op("t", lambda e: e.matmul(out=out, lhsT=lhsT, rhs=rhs,
                                                    start=start, stop=stop,
                                                    skip_group_check=True),
                            reads, writes, out=out)
            return K.op("t", lambda e: e.matmul(out=out, lhsT=lhsT, rhs=rhs,
                                                start=start, stop=stop,
                                                skip_group_check=True,
                                                tile_position=tp),
                        reads, writes, out=out)

        def tr(out, in_, ident, reads=(), writes=()):
            return K.op("t", lambda e: e.transpose(out=out, in_=in_, identity=ident),
                        tuple(reads) + (CONSTS,), writes, out=out)

        def vtt(out, a, b, op, reads=(), writes=()):
            return K.op("v", lambda e: e.tensor_tensor(out=out, in0=a, in1=b, op=op),
                        reads, writes, out=out)

        def vts(out, a, s1, op0, s2=None, op1=None, reads=(), writes=()):
            if op1 is None:
                return K.op("v", lambda e: e.tensor_scalar(
                    out=out, in0=a, scalar1=s1, scalar2=None, op0=op0),
                    reads, writes, out=out)
            return K.op("v", lambda e: e.tensor_scalar(
                out=out, in0=a, scalar1=s1, scalar2=s2, op0=op0, op1=op1),
                reads, writes, out=out)

        def vstt(out, in0, scalar, in1, op0, op1, reads=(), writes=()):
            return K.op("v", lambda e: e.scalar_tensor_tensor(
                out=out, in0=in0, scalar=scalar, in1=in1, op0=op0, op1=op1),
                reads, writes, out=out)

        def vcp(out, in_, reads=(), writes=()):
            return K.op("v", lambda e: e.tensor_copy(out=out, in_=in_),
                        reads, writes, out=out)

        def vred(out, in_, reads=(), writes=()):
            return K.op("v", lambda e: e.reduce_sum(out=out, in_=in_,
                                                    axis=mybir.AxisListType.X),
                        reads, writes, out=out)

        def vrec(out, in_, reads=(), writes=()):
            return K.op("v", lambda e: e.reciprocal(out=out, in_=in_),
                        reads, writes)

        def vrecf(out, in_, reads=(), writes=()):
            return K.op("v", lambda e: e.reciprocal_approx_fast(out=out, in_=in_),
                        reads, writes)

        def vscan(out, d0, d1, reads=(), writes=()):
            return K.op("v", lambda e: e.tensor_tensor_scan(
                out=out, data0=d0, data1=d1, initial=0.0,
                op0=ALU.add, op1=ALU.add), reads, writes, out=out)

        def vmemset(ap, val, reads=(), writes=()):
            return K.op("v", lambda e: e.memset(ap, val), reads, writes, out=ap)

        def act(out, in_, f, bias=None, scale=None, reads=(), writes=()):
            kw = {}
            if bias is not None:
                kw["bias"] = bias
            if scale is not None:
                kw["scale"] = scale
            return K.op("s", lambda e: e.activation(out=out, in_=in_, func=f, **kw),
                        tuple(reads) + (CONSTS,), writes, out=out)

        def scp(out, in_, reads=(), writes=()):
            return K.op("s", lambda e: e.copy(out=out, in_=in_), reads, writes,
                        out=out)

        def ptt(out, a, b, op, reads=(), writes=()):
            return K.op("p", lambda e: e.tensor_tensor(out=out, in0=a, in1=b, op=op),
                        reads, writes, out=out)

        def qkraw_s(sl): return qkraw[:, sl * (CH + 4):(sl + 1) * (CH + 4)]
        def vfmb_s(sl): return vfmb[:, 0:CH + 128]
        def knfm_g(g): return knfm[:, g * (CH + 128):(g + 1) * (CH + 128)]
        def xn_k(k): return xn[:, k * CH:(k + 1) * CH]
        def ktm_w(g, w): return ktm[:, (g * 9 + w) * 128:(g * 9 + w + 1) * 128]
        def vtm_w(g, w): return vtm[:, (g * 9 + w) * 128:(g * 9 + w + 1) * 128]
        def bsn_t(kv, g, j):
            o = ((kv * NG + g) * 9 + j) * 128
            return bsn[:, o:o + 128]
        def land_c(i): return land[:, i * CH:(i + 1) * CH]
        def landb_c(i): return landb[:, (i - 1) * CH:i * CH]
        def pst_g(g): return pst[:, g * CH:(g + 1) * CH]
        def rst_g(g): return rst[:, g * CH:(g + 1) * CH]
        def xst_g(g): return xst[:, g * CH:(g + 1) * CH]
        def onb_g(g): return onb[:, g * CH:(g + 1) * CH]
        def masks_g(g): return masks[:, g * CH:(g + 1) * CH]
        def ebc_g(g): return ebc[:, g * CH:(g + 1) * CH]
        def abcs_s(sl): return abcs[:, sl * CH:(sl + 1) * CH]
        def E_g(g): return inde[:, g * 128:(g + 1) * 128]
        def ind_g(g): return ind[:, g * 16:(g + 1) * 16]
        def tmA(src, g, j):
            return src(g, j // 2)
        def tm_block(src, g, j, hl):
            if hl == j % 2:
                return src(g, j // 2)
            return src(g, 4 + (j + 1) // 2)

        K.cap = []
        for dst, src in ((idf, idf_d), (idb, idb_d), (sel, sel_d), (trirow, trirow_d),
                         (ind, ind_d), (inde, inde_d), (ones1, ones1_d),
                         (epsrow, epsrow_d), (nrmw, nrmw_d), (bbc, bb_d),
                         (dtbc, dtb_d), (fconst, fconst_d),
                         (indl, indl_d), (eloc, eloc_d)):
            dma(dst[:], src, writes=(CONSTS,))
        for i, v in enumerate(FCONST_VALS):
            nc.const_aps.register(F32, v, fconst[:, i:i + 1])
        dma(colsc[0:16, 0:1], alog_d, writes=(rCOL,))
        for g in range(NG):
            dma(lamg[:, g:g + 1], lam_d[g * 128:(g + 1) * 128, :], writes=(CONSTS,))
            dma(wcq[:, g * 4:(g + 1) * 4], wcq_d[g * 128:(g + 1) * 128, :],
                writes=(CONSTS,))
            dma(wck[:, g * 4:(g + 1) * 4], wck_d[g * 128:(g + 1) * 128, :],
                writes=(CONSTS,))
        dma(stg[0:1, 0:D], bf_d, writes=(rSTG,))
        act(colsc[0:16, 1:2], colsc[0:16, 0:1], AF.Exp, reads=(rCOL,), writes=(rCOL,))
        vts(negea[:, :], colsc[0:16, 1:2], -1.0, ALU.mult, reads=(rCOL,),
            writes=(CONSTS,))
        vcp(brow[:, :], stg[0:1, 0:D], reads=(rSTG,), writes=(rBROW,))
        for g in range(NG):
            act(colsc[:, 2:3], lamg[:, g:g + 1], AF.Abs, reads=(CONSTS, rCOL),
                writes=(rCOL,))
            act(colsc[:, 3:4], colsc[:, 2:3], AF.Exp, scale=-1.0, reads=(rCOL,),
                writes=(rCOL,))
            act(colsc[:, 4:5], colsc[:, 3:4], AF.Ln, bias=1.0, reads=(rCOL,),
                writes=(rCOL,))
            vts(colsc[:, 5:6], lamg[:, g:g + 1], 0.0, ALU.max, reads=(CONSTS, rCOL),
                writes=(rCOL,))
            vtt(colsc[:, 6:7], colsc[:, 4:5], colsc[:, 5:6], ALU.add, reads=(rCOL,),
                writes=(rCOL,))
            vts(colsc[:, 7:8], colsc[:, 6:7], 0.25, ALU.add, reads=(rCOL,),
                writes=(rCOL,))
            vts(lamtile[:, g * 128:(g + 1) * 128], idf[:], colsc[:, 7:8], ALU.mult,
                reads=(CONSTS, rCOL), writes=(rLAM,))

        stg2b = stg2[:].bitcast(BF16)
        for wi, wd in enumerate((wq_d, wk_d, wv_d, wo_d)):
            for kk in range(8):
                dma(stg[:, 0:D], wd[kk * 128:(kk + 1) * 128, :], reads=(rSTG,),
                    writes=(rSTG,))
                vcp(stg2b[:, 0:D], stg[:, 0:D], reads=(rSTG,),
                    writes=(rSTG2, rRCQ[0], rRCQ[1], rRCQ[2], rRCQ[3]))
                if wi < 3:
                    ga, gb = (kk * 1024) // 1152, (kk * 1024 + 1023) // 1152
                    vcp(ktm[:, kk * 1024:(kk + 1) * 1024], stg2b[:, 0:D],
                        reads=(rSTG2,), writes=(rKT[ga], rKT[gb]))
                else:
                    dma(wo_s[kk * 128:(kk + 1) * 128, :], stg2b[:, 0:D],
                        reads=(rSTG2,), writes=(rWST,))
            if wi < 3:
                for m in range(8):
                    for kk in range(8):
                        off = kk * 1024 + m * 128
                        vcp(stg2b[:, 1024 + kk * 128:1024 + (kk + 1) * 128],
                            ktm[:, off:off + 128], reads=(rKT[off // 1152],),
                            writes=(rSTG2, rRCQ[0], rRCQ[1], rRCQ[2], rRCQ[3]))
                    dma(wqkvT_s[(wi * 8 + m) * 128:(wi * 8 + m + 1) * 128, :],
                        stg2b[:, 1024:2048], reads=(rSTG2,), writes=(rWST,))
        for kk in range(8):
            dma(stg[:, 0:H], wa_d[kk * 128:(kk + 1) * 128, :], reads=(rSTG,),
                writes=(rSTG,))
            dma(stg[:, H:2 * H], wb_d[kk * 128:(kk + 1) * 128, :], reads=(rSTG,),
                writes=(rSTG,))
            vcp(wab_sb[:, kk * 32:(kk + 1) * 32], stg[:, 0:32], reads=(rSTG,),
                writes=(rWAB,))

        dma(stg[0:4, 0:100], ld_d, writes=(rSTG,))
        dma(stg[0:50, 100:104], lu_d, writes=(rSTG,))
        for kk in range(8):
            dma(stg[:, 104:154], ua_d[kk * 128:(kk + 1) * 128, :], writes=(rSTG,))
            tr(pb[0][0:50, 0:128], stg[0:128, 104:154][:, 0:50], idf[:],
               reads=(rSTG,), writes=(PB[0],))
            vcp(stg[0:50, 154 + kk * 128:154 + (kk + 1) * 128], pb[0][0:50, 0:128],
                reads=(PB[0],), writes=(rSTG,))
        uaT = stg[0:50, 154:154 + D]
        mm(pb[1][0:4, 0:512], stg[0:50, 100:104], uaT[0:50, 0:512], True, True,
           reads=(rSTG,), writes=(PB[1],))
        mm(pb[2][0:4, 0:512], stg[0:50, 100:104], uaT[0:50, 512:1024], True, True,
           reads=(rSTG,), writes=(PB[2],))
        vcp(land[0:4, 0:512], pb[1][0:4, 0:512], reads=(PB[1],), writes=(rLND[0],))
        vcp(land[0:4, 512:1024], pb[2][0:4, 0:512], reads=(PB[2],), writes=(rLND[1],))
        mm(pb[1][0:100, 0:512], stg[0:4, 0:100], land[0:4, 0:512], True, True,
           reads=(rSTG, rLND[0]), writes=(PB[1],))
        mm(pb[2][0:100, 0:512], stg[0:4, 0:100], land[0:4, 512:1024], True, True,
           reads=(rSTG, rLND[1]), writes=(PB[2],))
        vcp(stg2[0:100, 0:512], pb[1][0:100, 0:512], reads=(PB[1],), writes=(rSTG2, rRCQ[0], rRCQ[1], rRCQ[2], rRCQ[3]))
        vcp(stg2[0:100, 512:1024], pb[2][0:100, 0:512], reads=(PB[2],), writes=(rSTG2, rRCQ[0], rRCQ[1], rRCQ[2], rRCQ[3]))
        dma(stg[0:100, 0:D], da_d, reads=(rSTG,), writes=(rSTG,))
        for it in range(8):
            for oh in range(2):
                mm(pb[3][0:128, 0:512], stg[0:100, it * 128:(it + 1) * 128],
                   stg2[0:100, oh * 512:(oh + 1) * 512], True, True,
                   reads=(rSTG, rSTG2), writes=(PB[3],))
                vcp(outsb[:, 0:512], pb[3][0:128, 0:512], reads=(PB[3],),
                    writes=(rSQ2[0],))
                for oi in range(4):
                    ot = oh * 4 + oi
                    dma(stg[:, 1128:1256], wf_d[ot * 128:(ot + 1) * 128,
                                                it * 128:(it + 1) * 128],
                        reads=(rSTG,), writes=(rSTG,))
                    tr(pb[5][0:128, 0:128], stg[:, 1128:1256], idf[:], reads=(rSTG,),
                       writes=(PB[5],))
                    vstt(stg[:].bitcast(BF16)[:, 2560:2688], pb[5][0:128, 0:128], 1.0,
                         outsb[:, oi * 128:(oi + 1) * 128],
                         ALU.mult, ALU.add, reads=(PB[5], rSQ2[0]),
                         writes=(rSTG,))
                    dma(weff_s[it * 128:(it + 1) * 128, ot * 128:(ot + 1) * 128],
                        stg[:].bitcast(BF16)[:, 2560:2688], reads=(rSTG,),
                        writes=(rWST,))

        for kv in range(2):
            for g in range(NG):
                vmemset(bsn_t(kv, g, 0)[:, :], 0.0, writes=(rBSN[kv][g],))
                for j in range(1, 9):
                    vmemset(bsn_t(kv, g, j)[:, :], 0.0, writes=(rBSN[kv][g],))
        vmemset(qhalo[:, :], 0.0, writes=(rHAL,))
        vmemset(khalo[:, :], 0.0, writes=(rHAL,))
        for sl in range(2):
            vmemset(qkraw_s(sl)[:, 0:4], 0.0, writes=(rQK[sl],))
        vmemset(vfmb_s(0)[:, 0:64], 0.0, writes=(rVF[0],))
        vmemset(vfmb_s(0)[:, 64 + CH:128 + CH], 0.0, writes=(rVF[0],))
        for g in range(NG):
            vmemset(knfm_g(g)[:, 0:64], 0.0, writes=(rKN[g],))
            vmemset(knfm_g(g)[:, 64 + CH:128 + CH], 0.0, writes=(rKN[g],))
        vmemset(zeros16[:, :], 0.0, writes=(CONSTS,))
        wdma(wmt[:, 0:1024], wqkvT_s[0:128, :], 0, reads=(rWST,),
             writes=(rWMT[0],))

        for ch in range(NCH):
            t0 = ch * CH
            for tt in range(NT):
                xs = stg[:, 0:D]
                K.wdma(lambda e, d_=xs, s_=x_d[t0 + tt * 128:
                                                t0 + (tt + 1) * 128, :]:
                       e.dma_start(out=d_, in_=s_), 7,
                       (rSTG,), (rSTG, rSTGQ[0], rSTGQ[1]), out=xs, q="s")
                vred(colsc[:, 0:1], xs, reads=(rSTG,), writes=(rCOL,))
                act(stg2[:, 0:D], xs, AF.Square, reads=(rSTG,), writes=(rSTG2, rRCQ[0], rRCQ[1], rRCQ[2], rRCQ[3]))
                vred(colsc[:, 1:2], stg2[:, 0:D], reads=(rSTG2,), writes=(rCOL,))
                vts(colsc[:, 2:3], colsc[:, 0:1], 1.0 / D, ALU.mult, reads=(rCOL,),
                    writes=(rCOL,))
                vts(colsc[:, 3:4], colsc[:, 1:2], 1.0 / D, ALU.mult, reads=(rCOL,),
                    writes=(rCOL,))
                vts(colsc[:, 4:5], colsc[:, 2:3], colsc[:, 2:3], ALU.mult,
                    reads=(rCOL,), writes=(rCOL,))
                vtt(colsc[:, 5:6], colsc[:, 3:4], colsc[:, 4:5], ALU.subtract,
                    reads=(rCOL,), writes=(rCOL,))
                act(colsc[:, 6:7], colsc[:, 5:6], AF.Ln, bias=1e-5, reads=(rCOL,),
                    writes=(rCOL,))
                act(colsc[:, 7:8], colsc[:, 6:7], AF.Exp, scale=-0.5, reads=(rCOL,),
                    writes=(rCOL,))
                vts(stg2[:].bitcast(BF16)[:, D:2 * D], xs, colsc[:, 2:3],
                    ALU.subtract, colsc[:, 7:8], ALU.mult, reads=(rSTG, rCOL),
                    writes=(rSTG2, rRCQ[0], rRCQ[1], rRCQ[2], rRCQ[3]))
                for kk in range(8):
                    bi = 6 + (kk % 2)
                    tr(pbb[bi][:, 0:128],
                       stg2[:].bitcast(BF16)[:, D + kk * 128:D + (kk + 1) * 128],
                       idb[:], reads=(rSTG2,), writes=(PB[bi],))
                    vcp(xn_k(kk)[:, tt * 128:(tt + 1) * 128], pbb[bi][:, 0:128],
                        reads=(PB[bi],), writes=(rXN[kk],))


            for li, off in ((0, 0), (1, 16)):
                ps = pb[0]
                for kk in range(8):
                    mm(ps[0:16, 0:512], wab_sb[:, kk * 32 + off:kk * 32 + off + 16],
                       xn_k(kk)[:, 0:CH], start=(kk == 0), stop=(kk == 7),
                       reads=(rWAB, rXN[kk]), writes=(PB[0],))
                vcp(land_c(li), ps[0:16, 0:512], reads=(PB[0],),
                    writes=(rLND[li],))

            la, lb = land_c(0), land_c(1)
            act(land_c(2), la, AF.Abs, bias=dtbc[:, :], reads=(rLND[0], CONSTS),
                writes=(rLND[2],))
            act(land_c(2), land_c(2), AF.Exp, scale=-1.0, reads=(rLND[2],),
                writes=(rLND[2],))
            act(land_c(2), land_c(2), AF.Ln, bias=1.0, reads=(rLND[2],),
                writes=(rLND[2],))
            vts(land_c(4), la, dtbc[:, :], ALU.add, 0.0, ALU.max,
                reads=(rLND[0], CONSTS), writes=(rLND[4],))
            vtt(land_c(3), land_c(4), land_c(2), ALU.add, reads=(rLND[4], rLND[2]),
                writes=(rLND[3],))
            vts(land_c(3), land_c(3), negea[:, :], ALU.mult, reads=(rLND[3], CONSTS),
                writes=(rLND[3],))
            act(land_c(4), lb, AF.Sigmoid, bias=bbc[:, :], reads=(rLND[1], CONSTS),
                writes=(rLND[4],))
            act(land_c(5), land_c(4), AF.Ln, reads=(rLND[4],), writes=(rLND[5],))
            for j in range(NJ):
                vscan(land_c(6)[:, j * 64:(j + 1) * 64],
                      land_c(3)[:, j * 64:(j + 1) * 64],
                      zeros16[:, :], reads=(rLND[3], CONSTS),
                      writes=(rLND[6],))
            vtt(land_c(7), land_c(6), land_c(5), ALU.subtract,
                reads=(rLND[6], rLND[5]),
                writes=(rLND[7],))
            act(landb_c(1), land_c(6), AF.Exp, reads=(rLND[6],),
                writes=(rLANDB, rAL[0]))
            vcp(landb_c(4)[:, 0:8], landb_c(1)[:, 63:CH:64],
                reads=(rLANDB, rAL[1]), writes=(rLANDB, rAL[0]))
            for j in range(NJ):
                vstt(land_c(0)[:, j * 64:(j + 1) * 64],
                     land_c(7)[:, j * 64:(j + 1) * 64], -1.0,
                     land_c(6)[:, 63 + j * 64:64 + j * 64]
                     .unsqueeze(2).broadcast_to([16, 1, 64]).squeeze(1),
                     ALU.mult, ALU.add, reads=(rLND[7], rLND[6]),
                     writes=(rLND[0],))
            act(land_c(0), land_c(0), AF.Exp, reads=(rLND[0],), writes=(rLND[0],))
            vcp(landb_c(2), land_c(6), reads=(rLND[6],), writes=(rLANDB, rAL[1]))
            vtt(land_c(1), land_c(6), landb_c(2), ALU.subtract,
                reads=(rLND[6], rLANDB), writes=(rLND[1],))
            vcp(landb_c(3), land_c(1), reads=(rLND[1],), writes=(rLANDB, rAL[1]))
            for tt in range(NT):
                tr(pb[4][0:128, 0:16], land_c(7)[0:16, tt * 128:(tt + 1) * 128],
                   idf[0:16, 0:16], reads=(rLND[7],), writes=(PB[4],))
                vcp(csT[:, tt * 16:(tt + 1) * 16], pb[4][0:128, 0:16],
                    reads=(PB[4],), writes=(rCST,))
                tr(pb[4][0:128, 16:32], land_c(0)[0:16, tt * 128:(tt + 1) * 128],
                   idf[0:16, 0:16], reads=(rLND[0],), writes=(PB[4],))
                vcp(facT[:, tt * 16:(tt + 1) * 16], pb[4][0:128, 16:32],
                    reads=(PB[4],), writes=(rFAC,))
            for m in range(4):
                for par in range(2):
                    for hl in range(2):
                        K.wdma(lambda e, dd=csS[hl * 64:(hl + 1) * 64,
                                                par + 2 * m:64:8],
                               ss=csT[par * 64:(par + 1) * 64,
                                      16 * m + hl:16 * (m + 1):2]:
                               e.dma_start(out=dd, in_=ss,
                                           allow_slow_non_contiguous=True),
                               6, (rCST,), (rCSS,), q="s")
            for g in range(NG):
                b = 6 + (g % 2)
                mm(pb[b][:, 0:512], E_g(g), landb_c(2)[0:16, 0:CH], True, False,
                   reads=(CONSTS, rLANDB), writes=(PB[b],))
                mm(pb[b][:, 0:512], E_g(g), landb_c(3)[0:16, 0:CH], False, False,
                   reads=(CONSTS, rLANDB), writes=(PB[b],))
                mm(pb[b][:, 0:512], sel[:], trirow[:], False, True,
                   reads=(CONSTS,), writes=(PB[b],))
                bcv = csS[:, g * 8:(g + 1) * 8].unsqueeze(2).broadcast_to([128, 8, 64])
                vtt(stg2[:, 0:CH], pb[b][:, 0:512], bcv, ALU.subtract,
                    reads=(PB[b], rCSS), writes=(rSTG2, rRCQ[0], rRCQ[1], rRCQ[2], rRCQ[3]))
                act(masks_g(g), stg2[:, 0:CH], AF.Exp, reads=(rSTG2,),
                    writes=(rMSK[g],))
            for g in range(NG):
                b = 6 + (g % 2)
                mm(pb[b][:, 0:512], E_g(g), landb_c(1)[0:16, 0:CH], True, True,
                   reads=(CONSTS, rLANDB), writes=(PB[b],))
                vcp(ebc_g(g), pb[b][:, 0:512], reads=(PB[b],), writes=(rEBC[g],))

            def proj_mt(mt_i, dst, dreg):
                sl = mt_i % 2
                buf = wmt[:, sl * 1024:(sl + 1) * 1024]
                ps = pb[0]
                for kk in range(8):
                    mm(ps[:, 0:512], buf[:, kk * 128:(kk + 1) * 128],
                       xn_k(kk)[:, 0:CH], start=(kk == 0), stop=(kk == 7),
                       reads=(rWMT[sl], rXN[kk]), writes=(PB[0],))
                nx = mt_i + 1
                if not (ch == NCH - 1 and nx >= 24):
                    wdma(wmt[:, (nx % 2) * 1024:(nx % 2 + 1) * 1024],
                         wqkvT_s[(nx % 24) * 128:(nx % 24 + 1) * 128, :], nx % 2,
                         reads=(rWST,), writes=(rWMT[nx % 2],))
                scp(dst, ps[:, 0:512], reads=(PB[0],), writes=(dreg,))

            def conv_block(which, g, sl):
                rb = qkraw_s(sl)
                halo = qhalo if which == "q" else khalo
                wcol = wcq if which == "q" else wck
                vcp(rb[:, 1:4], halo[:, g * 3:(g + 1) * 3], reads=(rHAL,),
                    writes=(rQK[sl],))
                accv = sqb[:, sl * 1024:sl * 1024 + CH]
                vts(accv, rb[:, 1:1 + CH], wcol[:, g * 4:g * 4 + 1], ALU.mult,
                    reads=(rQK[sl], CONSTS), writes=(rSQ2[sl],))
                for i in (1, 2, 3):
                    vstt(accv, rb[:, 1 + i:1 + i + CH],
                         wcol[:, g * 4 + i:g * 4 + i + 1], accv, ALU.mult,
                         ALU.add, reads=(rQK[sl], CONSTS, rSQ2[sl]),
                         writes=(rSQ2[sl],))
                vcp(halo[:, g * 3:(g + 1) * 3], rb[:, 513:516], reads=(rQK[sl],),
                    writes=(rHAL,))
                return accv, rSQ2[sl]

            qsq = {}
            for g in range(NG + 1):
                if g < NG:
                    sl = g % 2
                    proj_mt(g, qkraw_s(sl)[:, 4:4 + CH], rQK[sl])
                    accv, areg = conv_block("q", g, sl)
                    act(pst_g(g), accv, AF.Silu, reads=(areg,), writes=(rP[g],))
                    act(csc[:, sl * CH:sl * CH + CH], pst_g(g), AF.Square,
                        reads=(rP[g],), writes=(rCSC[sl],))
                    qsq[g] = sl
                if g > 0:
                    slp = qsq[g - 1]
                    mm(pb[2][0:16, 0:512], ind_g(g - 1),
                       csc[:, slp * CH:slp * CH + CH],
                       start=(g - 1 == 0), stop=(g - 1 == NG - 1),
                       reads=(CONSTS, rCSC[slp]), writes=(PB[2],))
            act(land_c(2), pb[2][0:16, 0:512], AF.Ln, bias=1e-6, reads=(PB[2],),
                writes=(rLND[2],))
            act(land_c(2), land_c(2), AF.Exp, scale=-0.5, reads=(rLND[2],),
                writes=(rLND[2],))
            vcp(landb_c(2), land_c(2), reads=(rLND[2],), writes=(rLANDB, rAL[1]))
            for g in range(NG):
                b = 6 + (g % 2)
                mm(pb[b][:, 0:512], E_g(g), landb_c(2)[0:16, 0:CH], True, True,
                   reads=(CONSTS, rLANDB), writes=(PB[b],))
                vtt(pst_g(g), pst_g(g), pb[b][:, 0:512], ALU.mult,
                    reads=(rP[g], PB[b]), writes=(rP[g],))
                vcp(rst_g(g), pst_g(g), reads=(rP[g],), writes=(rRS[g],))
            for g in range(NG + 1):
                if g < NG:
                    sl = g % 2
                    proj_mt(8 + g, qkraw_s(sl)[:, 4:4 + CH], rQK[sl])
                    accv, areg = conv_block("k", g, sl)
                    act(knfm_g(g)[:, 64:64 + CH], accv, AF.Silu, reads=(areg,),
                        writes=(rKN[g],))
                    act(csc[:, sl * CH:sl * CH + CH], knfm_g(g)[:, 64:64 + CH],
                        AF.Square, reads=(rKN[g],), writes=(rCSC[sl],))
                if g > 0:
                    slp = (g - 1) % 2
                    mm(pb[3][0:16, 0:512], ind_g(g - 1),
                       csc[:, slp * CH:slp * CH + CH],
                       start=(g - 1 == 0), stop=(g - 1 == NG - 1),
                       reads=(CONSTS, rCSC[slp]), writes=(PB[3],))
            act(land_c(4), pb[3][0:16, 0:512], AF.Ln, bias=1e-6, reads=(PB[3],),
                writes=(rLND[4],))
            act(land_c(4), land_c(4), AF.Exp, scale=-0.5, reads=(rLND[4],),
                writes=(rLND[4],))
            vcp(landb_c(3), land_c(4), reads=(rLND[4],), writes=(rLANDB, rAL[1]))
            for g in range(NG):
                b = 6 + (g % 2)
                mm(pb[b][:, 0:512], E_g(g), landb_c(3)[0:16, 0:CH], True, True,
                   reads=(CONSTS, rLANDB), writes=(PB[b],))
                vtt(knfm_g(g)[:, 64:64 + CH], knfm_g(g)[:, 64:64 + CH],
                    pb[b][:, 0:512], ALU.mult, reads=(rKN[g], PB[b]),
                    writes=(rKN[g],))
            for g in range(NG):
                for wi in range(9):
                    col = 64 + wi * 128 if wi < 4 else (wi - 4) * 128
                    bi = 6 + (wi % 2)
                    tr(pbb[bi][:, 0:128], knfm_g(g)[:, col:col + 128], idb[:],
                       reads=(rKN[g],), writes=(PB[bi],))
                    vcp(ktm_w(g, wi)[:, :], pbb[bi][:, 0:128], reads=(PB[bi],),
                        writes=(rKT[g],))
            for g in range(NG):
                sl = 0
                proj_mt(16 + g, vfmb_s(sl)[:, 64:64 + CH], rVF[sl])
                for wi in range(9):
                    col = 64 + wi * 128 if wi < 4 else (wi - 4) * 128
                    bi = 6 + (wi % 2)
                    tr(pbb[bi][:, 0:128], vfmb_s(0)[:, col:col + 128], idb[:],
                       reads=(rVF[0],), writes=(PB[bi],))
                    vcp(vtm_w(g, wi)[:, :], pbb[bi][:, 0:128], reads=(PB[bi],),
                        writes=(rVT[g],))
            for g in range(NG):
                vmemset(xst_g(g), 0.0, writes=(rX[g],))

            for g in range(NG):
                s_ = g % 2
                mm(pb[6 + s_][:, 0:8], E_g(g), landb_c(4)[0:16, 0:8], True, True,
                   reads=(CONSTS, rLANDB), writes=(PB[6 + s_],))
                vcp(csS[:, g * 8:(g + 1) * 8], pb[6 + s_][:, 0:8],
                    reads=(PB[6 + s_],), writes=(rCSS,))
            for j in range(NJ):
                ja, jb_ = j // 2, (j % 2) * 64
                for g in range(NG):
                    s_ = g % 2
                    for hl in range(2):
                        vts(khsc[jb_:jb_ + 64, s_ * 128 + hl * 64:
                                 s_ * 128 + (hl + 1) * 64],
                            ktm_w(g, ja)[jb_:jb_ + 64, hl * 64:(hl + 1) * 64],
                            facT[jb_:jb_ + 64, ja * 16 + 2 * g + hl:
                                 ja * 16 + 2 * g + hl + 1], ALU.mult,
                            reads=(rKT[g], rFAC), writes=(rKH[s_],))
                    for kv in range(2):
                        pgx = pb[2 * s_ + kv]
                        other = ktm_w if kv == 0 else vtm_w
                        oreg = rKT[g] if kv == 0 else rVT[g]
                        for hl in range(2):
                            mm(pgx[hl * 64:hl * 64 + 64, hl * 64:(hl + 1) * 64],
                               khsc[jb_:jb_ + 64, s_ * 128 + hl * 64:
                                    s_ * 128 + (hl + 1) * 64],
                               other(g, ja)[jb_:jb_ + 64, hl * 64:(hl + 1) * 64],
                               True, True, reads=(rKH[s_], oreg),
                               writes=(PB[2 * s_ + kv],), tp=(jb_, hl * 64))
                    for kv in range(2):
                        pgx = pb[2 * s_ + kv]
                        for hl in range(2):
                            vstt(bsn_t(kv, g, j + 1)[hl * 64:hl * 64 + 64,
                                                     hl * 64:(hl + 1) * 64],
                                 bsn_t(kv, g, j)[hl * 64:hl * 64 + 64,
                                                 hl * 64:(hl + 1) * 64],
                                 csS[hl * 64:hl * 64 + 64, g * 8 + j:g * 8 + j + 1],
                                 pgx[hl * 64:hl * 64 + 64, hl * 64:(hl + 1) * 64],
                                 ALU.mult, ALU.add,
                                 reads=(rBSN[kv][g], rCSS, PB[2 * s_ + kv]),
                                 writes=(rBSN[kv][g],))

            rcin = stg2[0:4, 0:1024]

            def al(s_):
                return landb[0:4, s_ * 1024:(s_ + 1) * 1024]

            def pr_sl(t, u):
                return t[:, 2 * u * CH:(2 * u + 2) * CH]

            lrs = lambda u: land[0:4, 2 * u * CH:(2 * u + 2) * CH]
            q1ctr = [0]

            def q1slot():
                q1ctr[0] += 1
                return q1ctr[0] % 2

            def q1s(sl):
                return q1b[:, sl * 1024:(sl + 1) * 1024]

            for u in range(4):
                g0 = 2 * u
                qsl = q1slot()
                vtt(q1s(qsl), pr_sl(rst, u), pr_sl(rst, u), ALU.mult,
                    reads=(rRS[g0], rRS[g0 + 1]), writes=(rQ1[qsl],))
                for h in range(2):
                    mm(pq[3][0:4, h * 512:h * 512 + 512], ones1[0:1, 0:4],
                       epsrow[:], True, False, reads=(CONSTS,),
                       writes=(PB[6 + h],))
                    mm(pq[3][0:4, h * 512:h * 512 + 512],
                       indl[:, h * 4:h * 4 + 4],
                       q1s(qsl)[:, h * 512:h * 512 + 512], False, True,
                       reads=(CONSTS, rQ1[qsl]), writes=(PB[6 + h],))
                vcp(lrs(u), pq[3][0:4, 0:1024], reads=(PB[6], PB[7]),
                    writes=(rLND[2 * u], rLND[2 * u + 1]))

            def M_pe1(u, kv):
                s_ = u % 2
                psap = pq[s_]
                bapr = (PB[2 * s_], PB[2 * s_ + 1])
                g0 = 2 * u
                srcT = pst if kv == 0 else xst
                sregs = (rP[g0], rP[g0 + 1]) if kv == 0 else (rX[g0], rX[g0 + 1])
                vtt(ptl[:, 0:1024], pr_sl(srcT, u), pr_sl(ebc, u), ALU.mult,
                    reads=sregs + (rEBC[g0], rEBC[g0 + 1]), writes=(rPTL[0],))
                for h in range(2):
                    g = g0 + h
                    for j in range(NJ):
                        mm(psap[:, h * 512 + j * 64:h * 512 + (j + 1) * 64],
                           bsn_t(kv, g, j)[:, :],
                           ptl[:, h * 512 + j * 64:h * 512 + (j + 1) * 64],
                           start=(j == 0), stop=False,
                           reads=(rBSN[kv][g], rPTL[0]), writes=(bapr[h],))
                    if kv == 0:
                        mm(psap[:, h * 512:h * 512 + 512],
                           lamtile[:, g * 128:(g + 1) * 128],
                           pr_sl(srcT, u)[:, h * 512:h * 512 + 512],
                           False, False, reads=(rLAM,) + sregs, writes=(bapr[h],))
                    for j in range(NJ):
                        for hl in range(2):
                            mm(pq[2][hl * 64:hl * 64 + 64,
                                     h * 512 + j * 64:h * 512 + (j + 1) * 64],
                               knfm_g(g)[hl * 64:hl * 64 + 64,
                                         64 + j * 64:64 + (j + 1) * 64],
                               srcT[hl * 64:hl * 64 + 64,
                                    g * CH + j * 64:g * CH + (j + 1) * 64],
                               True, True, reads=(rKN[g],) + sregs,
                               writes=(PB[4 + h],), tp=(hl * 64, hl * 64))

            def M_mask(u, kv):
                g0 = 2 * u
                vtt(csc[:, 0:1024], pq[2][:, 0:1024],
                    masks[:, g0 * CH:(g0 + 2) * CH], ALU.mult,
                    reads=(PB[4], PB[5], rMSK[g0], rMSK[g0 + 1]),
                    writes=(rCSC[0], rCSC[1]))

            def M_recon(u, kv):
                s_ = u % 2
                psap = pq[s_]
                bapr = (PB[2 * s_], PB[2 * s_ + 1])
                g0 = 2 * u
                wtm = ktm_w if kv == 0 else vtm_w
                for h in range(2):
                    g = g0 + h
                    wreg = (rKT[g] if kv == 0 else rVT[g])
                    for j in range(NJ):
                        for hl in range(2):
                            tile = tm_block(wtm, g, j, hl)
                            mm(psap[hl * 64:hl * 64 + 64,
                                    h * 512 + j * 64:h * 512 + (j + 1) * 64],
                               tile[hl * 64:hl * 64 + 64, hl * 64:(hl + 1) * 64],
                               csc[hl * 64:hl * 64 + 64,
                                   h * 512 + j * 64:h * 512 + (j + 1) * 64],
                               start=False, stop=(j == NJ - 1 and hl == 1),
                               reads=(wreg, rCSC[0], rCSC[1]), writes=(bapr[h],),
                               tp=(hl * 64, hl * 64))

            def M_pap(u):
                s_ = u % 2
                psap = pq[s_]
                bapr = (PB[2 * s_], PB[2 * s_ + 1])
                g0 = 2 * u
                qsl = q1slot()
                vtt(q1s(qsl), pr_sl(pst, u), psap[:, 0:1024], ALU.mult,
                    reads=(rP[g0], rP[g0 + 1]) + bapr, writes=(rQ1[qsl],))
                for h in range(2):
                    mm(pq[3][0:4, h * 512:h * 512 + 512],
                       indl[:, h * 4:h * 4 + 4],
                       q1s(qsl)[:, h * 512:h * 512 + 512], True, True,
                       reads=(CONSTS, rQ1[qsl]), writes=(PB[6 + h],))

            def A_alpha(u):
                s_ = u % 2
                act(rcin, pq[3][0:4, 0:1024], AF.Ln, bias=1e-12,
                    reads=(PB[6], PB[7]), writes=(rRCQ[0],))
                act(rcin, rcin, AF.Exp, scale=-1.0, reads=(rRCQ[0],),
                    writes=(rRCQ[0],))
                vtt(al(s_), lrs(u), rcin, ALU.mult,
                    reads=(rLND[2 * u], rLND[2 * u + 1], rRCQ[0]),
                    writes=(rAL[s_],))

            def B_bcast(u):
                s_ = u % 2
                g0 = 2 * u
                for h in range(2):
                    mm(pq[2][:, h * 512:h * 512 + 512],
                       eloc[0:4, h * 128:h * 128 + 128],
                       al(s_)[0:4, h * 512:h * 512 + 512], True, True,
                       reads=(CONSTS, rAL[s_]), writes=(PB[4 + h],))
                scp(abcs[:, 0:1024], pq[2][:, 0:1024], reads=(PB[4], PB[5]),
                    writes=(rABC[0],))

            def C_upd(u):
                s_ = u % 2
                psap = pq[s_]
                bapr = (PB[2 * s_], PB[2 * s_ + 1])
                g0 = 2 * u
                vtt(ptl[:, 0:1024], abcs[:, 0:1024], psap[:, 0:1024], ALU.mult,
                    reads=(rABC[0],) + bapr, writes=(rPTL[0],))
                vtt(pr_sl(rst, u), pr_sl(rst, u), ptl[:, 0:1024], ALU.subtract,
                    reads=(rRS[g0], rRS[g0 + 1], rPTL[0]),
                    writes=(rRS[g0], rRS[g0 + 1]))
                act(sqb[:, s_ * 1024:(s_ + 1) * 1024], pr_sl(rst, u),
                    AF.Square,
                    reads=(rRS[g0], rRS[g0 + 1]), writes=(rSQ2[s_],))
                qsl = q1slot()
                vtt(q1s(qsl), abcs[:, 0:1024], pr_sl(pst, u), ALU.mult,
                    reads=(rABC[0], rP[g0], rP[g0 + 1]), writes=(rQ1[qsl],))
                ptt(pr_sl(xst, u), pr_sl(xst, u), q1s(qsl), ALU.add,
                    reads=(rX[g0], rX[g0 + 1], rQ1[qsl]),
                    writes=(rX[g0], rX[g0 + 1]))

            def D_rs(u):
                s_ = u % 2
                g0 = 2 * u
                for h in range(2):
                    mm(pq[3][0:4, h * 512:h * 512 + 512], ones1[0:1, 0:4],
                       epsrow[:], True, False, reads=(CONSTS,),
                       writes=(PB[6 + h],))
                    mm(pq[3][0:4, h * 512:h * 512 + 512],
                       indl[:, h * 4:h * 4 + 4],
                       sqb[:, s_ * 1024 + h * 512:s_ * 1024 + h * 512 + 512],
                       False, True, reads=(CONSTS, rSQ2[s_]), writes=(PB[6 + h],))
                act(rcin, lrs(u), AF.Ln, reads=(rLND[2 * u], rLND[2 * u + 1]),
                    writes=(rRCQ[0],))
                act(rcin, rcin, AF.Exp, scale=-1.0, reads=(rRCQ[0],),
                    writes=(rRCQ[0],))

            def E_beta(u):
                s_ = u % 2
                g0 = 2 * u
                vtt(al(s_), pq[3][0:4, 0:1024], rcin, ALU.mult,
                    reads=(PB[6], PB[7], rRCQ[0]), writes=(rAL[s_],))
                vcp(lrs(u), pq[3][0:4, 0:1024], reads=(PB[6], PB[7]),
                    writes=(rLND[2 * u], rLND[2 * u + 1]))
                for h in range(2):
                    mm(pq[2][:, h * 512:h * 512 + 512],
                       eloc[0:4, h * 128:h * 128 + 128],
                       al(s_)[0:4, h * 512:h * 512 + 512], True, True,
                       reads=(CONSTS, rAL[s_]), writes=(PB[4 + h],))
                scp(abcs[:, 0:1024], pq[2][:, 0:1024], reads=(PB[4], PB[5]),
                    writes=(rABC[0],))
                vtt(ptl[:, 0:1024], abcs[:, 0:1024], pr_sl(pst, u), ALU.mult,
                    reads=(rABC[0], rP[g0], rP[g0 + 1]), writes=(rPTL[0],))
                vtt(pr_sl(pst, u), pr_sl(rst, u), ptl[:, 0:1024], ALU.add,
                    reads=(rRS[g0], rRS[g0 + 1], rPTL[0]),
                    writes=(rP[g0], rP[g0 + 1]))

            units = [u for it in range(CG_ITERS) for u in range(4)]
            pu = None
            for u in units:
                if pu is not None:
                    B_bcast(pu)
                M_pe1(u, 0)
                if pu is not None:
                    C_upd(pu)
                M_mask(u, 0)
                M_recon(u, 0)
                if pu is not None:
                    D_rs(pu)
                    E_beta(pu)
                M_pap(u)
                A_alpha(u)
                pu = u
            B_bcast(pu)
            C_upd(pu)
            D_rs(pu)
            E_beta(pu)
            pu = None
            for u in range(4):
                M_pe1(u, 1)
                M_mask(u, 1)
                M_recon(u, 1)
                s_ = u % 2
                psap = pq[s_]
                bapr = (PB[2 * s_], PB[2 * s_ + 1])
                g0 = 2 * u
                act(q1b[:, 0:1024], psap[:, 0:1024], AF.Square, reads=bapr,
                    writes=(rQ1[0],))
                for h in range(2):
                    mm(pq[3][0:16, 0:512], ind_g(g0 + h),
                       q1b[:, h * 512:h * 512 + 512], start=(g0 + h == 0),
                       stop=(g0 + h == NG - 1), reads=(CONSTS, rQ1[0]),
                       writes=(PB[6],))
                scp(pr_sl(onb, u), psap[:, 0:1024], reads=bapr,
                    writes=(rON[g0], rON[g0 + 1]))
            vts(land_c(2), pq[3][0:16, 0:512], 1.0 / 64, ALU.mult, reads=(PB[6],),
                writes=(rLND[2],))
            act(land_c(2), land_c(2), AF.Ln, bias=1e-5, reads=(rLND[2],),
                writes=(rLND[2],))
            act(land_c(2), land_c(2), AF.Exp, scale=-0.5, reads=(rLND[2],),
                writes=(rLND[2],))
            vcp(landb_c(2), land_c(2), reads=(rLND[2],), writes=(rLANDB, rAL[1]))
            for g in range(NG):
                b = 4 + (g % 2)
                mm(pb[b][:, 0:512], E_g(g), landb_c(2)[0:16, 0:CH], True, True,
                   reads=(CONSTS, rLANDB), writes=(PB[b],))
                vstt(onb_g(g), onb_g(g), nrmw[:, :], pb[b][:, 0:512], ALU.mult,
                     ALU.mult, reads=(rON[g], CONSTS, PB[b]), writes=(rON[g],))

            def wrl(i):
                src = wo_s if i < 8 else weff_s
                r = i % 8
                wdma(wrhs[:, (i % 3) * 1024:(i % 3 + 1) * 1024],
                     src[r * 128:(r + 1) * 128, :], 2 + (i % 3),
                     reads=(rWST,), writes=(rWRHS[i % 3],))
            for i in range(3):
                wrl(i)
            for i in range(16):
                sl = i % 3
                for tt in range(NT):
                    for mh in range(2):
                        mm(pb[tt * 2 + mh][:, 0:512],
                           (onb_g(i) if i < 8 else xn_k(i - 8))
                           [:, tt * 128:(tt + 1) * 128],
                           wrhs[:, sl * 1024 + mh * 512:sl * 1024 + (mh + 1) * 512],
                           start=(i == 0), stop=False,
                           reads=((rON[i] if i < 8 else rXN[i - 8]), rWRHS[sl]),
                           writes=(PB[tt * 2 + mh],))
                if i + 3 < 16:
                    wrl(i + 3)
            for mh in range(2):
                for tt in range(NT):
                    mm(pb[tt * 2 + mh][:, 0:512], ones1[0:1, 0:128],
                       brow[0:1, mh * 512:(mh + 1) * 512], start=False, stop=True,
                       reads=(CONSTS, rBROW), writes=(PB[tt * 2 + mh],))
                    wdma(stg[:, (tt % 2) * CH:(tt % 2 + 1) * CH],
                         x_d[t0 + tt * 128:t0 + (tt + 1) * 128,
                             mh * 512:(mh + 1) * 512], 10 + tt % 2,
                         reads=(),
                         writes=(rSTG, rSTGQ[tt % 2]))
                    osl = (mh * NT + tt) % 2
                    vtt(sqbf[:, osl * CH:(osl + 1) * CH],
                        pb[tt * 2 + mh][:, 0:512],
                        stg[:, (tt % 2) * CH:(tt % 2 + 1) * CH],
                        ALU.add, reads=(PB[tt * 2 + mh], rSTGQ[tt % 2]),
                        writes=(rSQ2[osl],))
                    wdma(out_d[t0 + tt * 128:t0 + (tt + 1) * 128,
                               mh * 512:(mh + 1) * 512],
                         sqbf[:, osl * CH:(osl + 1) * CH], 8 + osl,
                         reads=(rSQ2[osl],), writes=())

            for kv in range(2):
                for g in range(NG):
                    vcp(bsn_t(kv, g, 0)[:, :], bsn_t(kv, g, 8)[:, :],
                        reads=(rBSN[kv][g],), writes=(rBSN[kv][g],))

        recs = K.cap
        K.cap = None
        order = _schedule(recs) if SCHEDULE else range(len(recs))
        for idx in order:
            e, slot, fn, reads, writes, out = recs[idx]
            if e.startswith("w"):
                K.wdma(fn, slot, reads, writes, q=e[1])
            else:
                K.op(e, fn, reads, writes)

        K.q["d"].append(lambda e, v=K.cnt["d"]: e.wait_ge(sems["d"], v))
        for osl in range(2):
            K.q["d"].append(lambda e, sm=sems[f"w{8 + osl}"],
                            v=K.wcnt[8 + osl]: e.wait_ge(sm, v))
        K.emit(blk)
    return nc


def _consts():
    idf = np.eye(128, dtype=np.float32)
    sel = np.zeros((64, 128), np.float32)
    for k in range(64):
        sel[k, k] = 1.0
        sel[k, 64 + k] = 1.0
    tri = np.zeros((64, CH), np.float32)
    for sl in range(64):
        for j in range(NJ):
            tri[sl, j * 64:j * 64 + sl] = -BIGNEG
    tri = np.zeros((64, CH), np.float32)
    for sl in range(64):
        for j in range(NJ):
            tri[sl, j * 64:j * 64 + sl] = BIGNEG
    ind = np.zeros((128, 128), np.float32)
    inde = np.zeros((16, NG * 128), np.float32)
    for g in range(NG):
        for p in range(128):
            ind[p, g * 16 + 2 * g + p // 64] = 1.0
            inde[2 * g + p // 64, g * 128 + p] = 1.0
    ones1 = np.ones((1, 128), np.float32)
    epsrow = np.full((1, CH), 1e-12, np.float32)
    indl = np.zeros((128, 8), np.float32)
    eloc = np.zeros((4, 256), np.float32)
    for p in range(128):
        indl[p, 0 + p // 64] = 1.0
        indl[p, 4 + 2 + p // 64] = 1.0
        eloc[0 + p // 64, p] = 1.0
        eloc[2 + p // 64, 128 + p] = 1.0
    return dict(IDF=idf, IDB=np.eye(128).astype(bfloat16),
                SEL=sel.astype(bfloat16), TRIROW=tri.astype(bfloat16),
                IND=ind.astype(bfloat16), INDE=inde.astype(bfloat16),
                ONES1=ones1.astype(bfloat16), EPSROW=epsrow.astype(bfloat16),
                FCONST=np.tile(np.array([FCONST_VALS], np.float32), (128, 1)),
                INDL=indl.astype(bfloat16), ELOC=eloc.astype(bfloat16))


def _in_map(core, inputs, T):
    c = _consts()
    hs = np.ascontiguousarray(np.asarray(inputs["L0_attn_hidden_states"],
                                         np.float32)[core, :T])
    return dict(
        x=hs,
        Wq=np.asarray(inputs["L0_attn_Wq"], np.float32),
        Wk=np.asarray(inputs["L0_attn_Wk"], np.float32),
        Wv=np.asarray(inputs["L0_attn_Wv"], np.float32),
        Wa=np.asarray(inputs["L0_attn_Wa"], np.float32),
        Wb=np.asarray(inputs["L0_attn_Wb"], np.float32),
        bb=np.asarray(inputs["L0_attn_bb"], np.float32).reshape(H, 1),
        A_log=np.asarray(inputs["L0_attn_A_log"], np.float32).reshape(H, 1),
        dt_bias=np.asarray(inputs["L0_attn_dt_bias"], np.float32).reshape(H, 1),
        lambda_params=np.asarray(inputs["L0_attn_lambda_params"],
                                 np.float32).reshape(D, 1),
        Wconv_q=np.asarray(inputs["L0_attn_Wconv_q"], np.float32),
        Wconv_k=np.asarray(inputs["L0_attn_Wconv_k"], np.float32),
        norm_w=np.tile(np.asarray(inputs["L0_attn_norm_w"], np.float32),
                       2).reshape(128, 1),
        Wo=np.asarray(inputs["L0_attn_Wo"], np.float32),
        W_ffn=np.asarray(inputs["L0_ffn_W"], np.float32),
        b_ffn=np.asarray(inputs["L0_ffn_b"], np.float32).reshape(1, D),
        lora_down=np.asarray(inputs["L0_ffn_lora_down"], np.float32),
        lora_up=np.asarray(inputs["L0_ffn_lora_up"], np.float32),
        down_aux=np.asarray(inputs["L0_ffn_down_aux"], np.float32),
        up_aux=np.asarray(inputs["L0_ffn_up_aux"], np.float32),
        **c)


def kernel(**inputs):
    T = 4096
    nc = build_nc(T)
    in_maps = [_in_map(i, inputs, T) for i in range(NCORES)]
    res = run_bass_kernel_spmd(nc, in_maps, core_ids=list(range(NCORES)))
    out = np.stack([np.asarray(res.results[i]["out"], np.float32)
                    for i in range(NCORES)], axis=0)
    return out
